# version 01 (final)
# baseline (speedup 1.0000x reference)
import os
os.environ.setdefault("TRN_ALLOW_ICI_ALL_TO_ALL", "1")

import numpy as np

import concourse.bass as bass
import concourse.mybir as mybir
from concourse.bass_utils import run_bass_kernel_spmd
from concourse.aot_env import allow_ici_all_to_all

F32 = mybir.dt.float32
AF = mybir.ActivationFunctionType
ALU = mybir.AluOpType
AX = mybir.AxisListType.X

T = 2048
DM = 1024
NH = 16
DH = 64
NT = T // 128
NDC = DM // 128
NCORE = 8

IDX200 = np.linspace(0.0, 2047.0, 200).astype(np.int64)


ENGS = ("t", "v", "s", "g", "y")


class Sem:
    __slots__ = ("h", "n", "name")

    def __init__(self, h, name):
        self.h = h
        self.n = 0
        self.name = name

    def tok(self):
        return (self, self.n)


class K:

    def __init__(self, nc):
        self.nc = nc
        self.q = {e: [] for e in ENGS}
        self.memo = {e: {} for e in ENGS}

    def sem(self, name):
        return Sem(self.nc.alloc_semaphore(name=name), name)

    def wait(self, eng, tok):
        sem, val = tok
        if val <= 0:
            return
        m = self.memo[eng]
        if m.get(id(sem), -1) >= val:
            return
        m[id(sem)] = val
        self.q[eng].append(lambda e, s=sem, v=val: e.wait_ge(s.h, v))

    def waits(self, eng, toks):
        for t in toks:
            if t is not None:
                self.wait(eng, t)

    def emit(self, eng, f, inc=None, k=1, after=()):
        self.waits(eng, after)
        if inc is not None:
            inc.n += k
            self.q[eng].append(lambda e, f=f, s=inc, kk=k: f(e).then_inc(s.h, kk))
            return inc.tok()
        self.q[eng].append(lambda e, f=f: f(e))
        return None

    def dma(self, eng, out, in_, inc, after=(), **kw):
        return self.emit(eng, lambda e, o=out, i=in_, kw=kw: e.dma_start(out=o, in_=i, **kw),
                         inc=inc, k=16, after=after)

    def run(self, block):
        for name, attr in (("t", "tensor"), ("v", "vector"), ("s", "scalar"),
                           ("g", "gpsimd"), ("y", "sync")):
            ops = self.q[name]
            getattr(block, attr)(lambda e, ops=ops: [f(e) for f in ops])



def prep_inputs(inp):
    c = {}
    A = lambda x: np.ascontiguousarray(x, dtype=np.float32)

    c["a0_wqT"] = A(inp["L0_attn_Wq"].T)
    c["a0_wkT"] = A(inp["L0_attn_Wk"].T)
    c["a0_wvT"] = A(inp["L0_attn_Wv"].T)
    c["a0_woT"] = A(inp["L0_attn_Wo"].T)
    c["a0_bqT"] = A(np.asarray(inp["L0_attn_bq"]).reshape(NH, DH).T)
    c["a0_bkT"] = A(np.asarray(inp["L0_attn_bk"]).reshape(NH, DH).T)
    c["a0_bvT"] = A(np.asarray(inp["L0_attn_bv"]).reshape(NH, DH).T)
    c["a0_bo"] = A(np.asarray(inp["L0_attn_bo"]).reshape(1, DM))
    c["a0_bv_row"] = A(np.asarray(inp["L0_attn_bv"]).reshape(1, DM))
    tk = np.asarray(inp["L0_attn_rel_k_emb"])
    c["a0_tkT_rev"] = A(tk[::-1].T)
    tv = np.asarray(inp["L0_attn_rel_v_emb"])
    tvP = tv[1:256][::-1]
    c["a0_tvP"] = A(np.concatenate([tvP, np.zeros((1, DH), np.float32)], 0))
    c["a0_tv_edge"] = A(np.stack([tv[256], tv[0]]))

    c["f_wqkv"] = A(inp["L0_ffn_Wqkv"])
    c["f_qs"] = A(inp["L0_ffn_q_scale"])
    c["f_ks"] = A(inp["L0_ffn_k_scale"])
    c["f_wo"] = A(inp["L0_ffn_Wo"])
    c["f_lng"] = A(inp["L0_ffn_ln_g"])
    c["f_lnb"] = A(inp["L0_ffn_ln_b"])
    W1 = np.asarray(inp["L0_ffn_W1"])
    pad = np.zeros((6, DM, 86), np.float32)
    c["f_w1a"] = A(np.concatenate([W1[:, :, :2730], pad], 2))
    c["f_w1g"] = A(np.concatenate([W1[:, :, 2730:], pad], 2))
    b1 = np.asarray(inp["L0_ffn_b1"])
    bpad = np.zeros((6, 86), np.float32)
    b1a = np.concatenate([b1[:, :2730], bpad], 1)
    b1g = np.concatenate([b1[:, 2730:], bpad], 1)
    c["f_b1aT"] = A(b1a.reshape(6, 22, 128).transpose(0, 2, 1))
    c["f_b1gT"] = A(b1g.reshape(6, 22, 128).transpose(0, 2, 1))
    W2 = np.asarray(inp["L0_ffn_W2"])
    c["f_w2"] = A(np.concatenate([W2, np.zeros((6, 86, DM), np.float32)], 1))
    c["f_b2"] = A(inp["L0_ffn_b2"])

    half = np.arange(0, DH, 2, dtype=np.float64)
    inv_freq = 1.0 / 10000.0 ** (half / DH)
    t = np.arange(T, dtype=np.float64)
    freqs = np.concatenate([t[:, None] * inv_freq[None, :]] * 2, -1)
    base_scale = (half + 0.4 * DH) / (1.4 * DH)
    power = (t - T // 2) / (256 // 2)
    scale = np.concatenate([base_scale[None, :] ** power[:, None]] * 2, -1)
    c["rot_cs"] = A(np.cos(freqs) * scale)
    c["rot_ss"] = A(np.sin(freqs) * scale)
    c["rot_ci"] = A(np.cos(freqs) / scale)
    c["rot_si"] = A(np.sin(freqs) / scale)

    c["a1_wq"] = A(inp["L1_attn_Wq"])
    c["a1_bq_row"] = A(np.asarray(inp["L1_attn_bq"]).reshape(1, DM))
    c["a1_wk"] = A(inp["L1_attn_Wk"])
    c["a1_bk_row"] = A(np.asarray(inp["L1_attn_bk"]).reshape(1, DM))
    c["a1_we"] = A(inp["L1_attn_We"])
    c["a1_wr"] = A(inp["L1_attn_Wr"])
    c["a1_wc"] = A(inp["L1_attn_Wc"])
    c["a1_bc"] = A(np.asarray(inp["L1_attn_bc"]).reshape(1, DH))
    c["a1_wo"] = A(inp["L1_attn_Wo"])
    c["a1_bo"] = A(np.asarray(inp["L1_attn_bo"]).reshape(1, DM))

    c["g1_wqkv"] = A(inp["L1_ffn_W_qkv"])
    c["g1_qg"] = A(np.asarray(inp["L1_ffn_g_info"])[0, 0].reshape(NH, DH))
    c["g1_wproj"] = A(inp["L1_ffn_W_proj"])
    c["g1_bproj"] = A(np.asarray(inp["L1_ffn_b_proj"]).reshape(1, DM))

    x0 = np.asarray(inp["L0_attn_hidden_states"], dtype=np.float32)
    return [dict(c, x0=A(x0[b])) for b in range(NCORE)]



def build_nc(upto="all"):
    nc = bass.Bass("TRN2", target_bir_lowering=False, debug=False)

    def dext(name, shape):
        return nc.dram_tensor(name, list(shape), F32, kind="ExternalInput")

    def dint(name, shape):
        return nc.dram_tensor(name, list(shape), F32, kind="Internal")

    H = {}
    for name, shape in [
        ("x0", (T, DM)),
        ("a0_wqT", (DM, DM)), ("a0_wkT", (DM, DM)), ("a0_wvT", (DM, DM)), ("a0_woT", (DM, DM)),
        ("a0_bqT", (DH, NH)), ("a0_bkT", (DH, NH)), ("a0_bvT", (DH, NH)), ("a0_bo", (1, DM)),
        ("a0_bv_row", (1, DM)),
        ("a0_tkT_rev", (DH, 257)), ("a0_tvP", (256, DH)), ("a0_tv_edge", (2, DH)),
        ("f_wqkv", (6, DM, 3 * DM)), ("f_qs", (6, DH)), ("f_ks", (6, DH)),
        ("f_wo", (6, DM, DM)), ("f_lng", (6, DM)), ("f_lnb", (6, DM)),
        ("f_w1a", (6, DM, 2816)), ("f_w1g", (6, DM, 2816)),
        ("f_b1aT", (6, 128, 22)), ("f_b1gT", (6, 128, 22)),
        ("f_w2", (6, 2816, DM)), ("f_b2", (6, DM)),
        ("rot_cs", (T, DH)), ("rot_ss", (T, DH)), ("rot_ci", (T, DH)), ("rot_si", (T, DH)),
        ("a1_wq", (DM, DM)), ("a1_bq_row", (1, DM)), ("a1_wk", (DM, DM)), ("a1_bk_row", (1, DM)),
        ("a1_we", (NH, 200, 20)), ("a1_wr", (NH, 200, 20)),
        ("a1_wc", (40, DH)), ("a1_bc", (1, DH)), ("a1_wo", (DM, DM)), ("a1_bo", (1, DM)),
        ("g1_wqkv", (DM, 3 * DM)), ("g1_qg", (NH, DH)),
        ("g1_wproj", (DM, DM)), ("g1_bproj", (1, DM)),
    ]:
        H[name] = dext(name, shape)
    H["out"] = nc.dram_tensor("out", [T, DM], F32, kind="ExternalOutput")

    for name, shape in [
        ("s_xn0", (T, DM)), ("s_xnT", (DM, T)), ("s_ffn0", (T, DM)),
        ("s_x1", (T, DM)), ("s_xn1", (T, DM)),
        ("s_eshear", (2, T * 511 + 8)),
        ("s_u_in", (NCORE, 2, T, 512)), ("s_u_out", (NCORE, 2, T, 512)),
        ("s_w2_in", (NCORE, 2, T, DH)), ("s_w2_out", (NCORE, 2, T, DH)),
        ("s_l1q", (NH, DH, T)), ("s_l1k", (NH, DH, T)), ("s_l1v", (NH, T, DH)),
        ("s_ar_in", (1, 2)), ("s_ar_out", (1, 2)), ("s_row", (4, T)),
    ]:
        H[name] = dint(name, shape)

    ident = nc.inline_tensor(np.eye(128, dtype=np.float32), name="c_ident")
    konst_np = np.zeros((128, 8), np.float32)
    konst_np[:, 0] = 1e-5
    konst_np[:, 1] = 1e-12
    konst_np[:, 2] = -3.4028234663852886e38
    konst_np[:, 3] = 1.0
    konst = nc.inline_tensor(konst_np, name="c_konst")
    ones128 = nc.inline_tensor(np.ones((128, 128), np.float32), name="c_ones")
    tri_l = nc.inline_tensor(np.tril(np.ones((128, 128), np.float32)), name="c_tril")
    tri_u = nc.inline_tensor(np.triu(np.ones((128, 128), np.float32)), name="c_triu")
    lam_np = np.zeros((8, 128, 128), np.float32)
    for j in range(2):
        for kc in range(4):
            q_rel = j * 128 + np.arange(128)[None, :]
            k_rel = kc * 128 + np.arange(128)[:, None] - 256
            lam_np[j * 4 + kc] = ((k_rel <= q_rel) & (k_rel >= q_rel - 256)).astype(np.float32)
    lam = nc.inline_tensor(np.ascontiguousarray(
        lam_np.transpose(1, 0, 2).reshape(128, 8 * 128)), name="c_lam")

    def sb(name, shape):
        return nc.alloc_sbuf_tensor(name, list(shape), F32)

    XA = sb("XA", (128, 6656))
    RES = sb("RES", (128, NT * DM))
    BIG = sb("BIG", (128, NT * DM))
    MID = sb("MID", (128, 4608))
    SMALL = sb("SMALL", (128, 6144))
    CONST = sb("CONST", (128, 1544))
    STAT = sb("STAT", (128, 160))

    C_IDENT, C_KONST, C_TRIL, C_TRIU, C_LAM, C_ONES = 0, 128, 136, 264, 392, 1416

    PS = [nc.alloc_psum_tensor(f"ps{i}", [128, 512], F32) for i in range(8)]

    k = K(nc)
    s_tc = k.sem("tc")
    s_vc = k.sem("vc")
    s_sk = k.sem("sk")
    s_gc = k.sem("gc")
    s_cc = k.sem("cc")

    ctx = dict(nc=nc, k=k, H=H, PS=PS, XA=XA, RES=RES, BIG=BIG, MID=MID,
               SMALL=SMALL, CONST=CONST, STAT=STAT,
               C_IDENT=C_IDENT, C_KONST=C_KONST, C_TRIL=C_TRIL, C_TRIU=C_TRIU, C_LAM=C_LAM,
               s_tc=s_tc, s_vc=s_vc, s_sk=s_sk,
               s_gc=s_gc, s_cc=s_cc,
               C_ONES=C_ONES,
               const_loads=[(C_IDENT, 128, ident), (C_KONST, 8, konst),
                            (C_TRIL, 128, tri_l), (C_TRIU, 128, tri_u),
                            (C_LAM, 1024, lam), (C_ONES, 128, ones128)])

    emit_all(ctx, upto)

    with nc.Block() as block:
        k.run(block)
    return nc


def ln_tile(ctx, src_ap, dst_ap, stat_col, disc_col, after=()):
    k, STAT, XA, CONST, C_KONST = ctx["k"], ctx["STAT"], ctx["XA"], ctx["CONST"], ctx["C_KONST"]
    s_vc, s_sk = ctx["s_vc"], ctx["s_sk"]
    sm = STAT[:, stat_col:stat_col + 1]
    mu = STAT[:, stat_col + 1:stat_col + 2]
    vv = STAT[:, stat_col + 2:stat_col + 3]
    rs = STAT[:, stat_col + 3:stat_col + 4]
    disc = XA[:, disc_col:disc_col + DM]
    t1 = k.emit("v", lambda e: e.reduce_sum(sm, src_ap, axis=AX), inc=s_vc, after=after)
    t2 = k.emit("v", lambda e: e.tensor_scalar(mu, sm, 1.0 / DM, None, ALU.mult), inc=s_vc, after=[t1])
    t3 = k.emit("v", lambda e: e.tensor_scalar(dst_ap, src_ap, mu, None, ALU.subtract), inc=s_vc, after=[t2])
    t4 = k.emit("v", lambda e: e.scalar_tensor_tensor(disc, dst_ap, 1.0, dst_ap,
                                                      ALU.mult, ALU.mult, accum_out=vv),
                inc=s_vc, after=[t3])
    t5 = k.emit("v", lambda e: e.tensor_scalar(vv, vv, 1.0 / DM, CONST[:, C_KONST:C_KONST + 1],
                                               ALU.mult, ALU.add), inc=s_vc, after=[t4])
    t6 = k.emit("s", lambda e: e.activation(rs, vv, AF.Sqrt), inc=s_sk, after=[t5])
    t7 = k.emit("v", lambda e: e.reciprocal(rs, rs), inc=s_vc, after=[t6])
    t8 = k.emit("v", lambda e: e.tensor_scalar(dst_ap, dst_ap, rs, None, ALU.mult), inc=s_vc, after=[t7])
    return t8


def emit_all(ctx, upto):
    k = ctx["k"]
    H = ctx["H"]
    PS, XA, RES, BIG, CONST = ctx["PS"], ctx["XA"], ctx["RES"], ctx["BIG"], ctx["CONST"]
    s_tc, s_vc, s_sk = ctx["s_tc"], ctx["s_vc"], ctx["s_sk"]
    C_IDENT = ctx["C_IDENT"]
    ident_ap = CONST[:, C_IDENT:C_IDENT + 128]

    s_c0 = k.sem("p0_ld")
    s_xs = [k.sem("p0_xs0"), k.sem("p0_xs1")]
    s_xnTs = k.sem("p0_xnTs")

    for dst_c, width, handle in ctx["const_loads"]:
        k.dma("y", CONST[:, dst_c:dst_c + width], handle.ap(), s_c0)
    for tt in range(NT):
        k.dma("y", RES[:, tt * DM:(tt + 1) * DM], H["x0"].ap()[tt * 128:(tt + 1) * 128, :], s_c0)
    tok_x0 = s_c0.tok()

    slot_free = [None, None]
    bank_free = [None, None]
    xnT_done = []
    for tt in range(NT):
        par = tt % 2
        xn_slot = XA[:, 2048 + par * DM:2048 + (par + 1) * DM]
        src = RES[:, tt * DM:(tt + 1) * DM]
        aft = [tok_x0] + (slot_free[par] or [])
        tln = ln_tile(ctx, src, xn_slot, 4 * par, par * 1024, after=aft)
        tok_store = k.dma("g", H["s_xn0"].ap()[tt * 128:(tt + 1) * 128, :], xn_slot,
                          s_xs[par], after=[tln])
        evs = []
        for dc in range(NDC):
            bp = dc % 2
            bank = PS[bp]
            ttr = k.emit("t", lambda e, s=xn_slot, dc=dc, b=bank: e.transpose(
                b[:, 0:128], s[:, dc * 128:(dc + 1) * 128], ident_ap),
                inc=s_tc, after=[tln, bank_free[bp]] if bank_free[bp] else [tln])
            dcol = dc * T + tt * 128
            eng, ch = ("v", s_vc) if dc % 2 == 0 else ("s", s_sk)
            if eng == "v":
                tev = k.emit(eng, lambda e, b=bank, dcol=dcol: e.tensor_copy(
                    BIG[:, dcol:dcol + 128], b[:, 0:128]), inc=ch, after=[ttr])
            else:
                tev = k.emit(eng, lambda e, b=bank, dcol=dcol: e.copy(
                    BIG[:, dcol:dcol + 128], b[:, 0:128]), inc=ch, after=[ttr])
            bank_free[bp] = tev
            evs.append(tev)
        xnT_done.append(evs)
        slot_free[par] = [tok_store, s_tc.tok()]

    for dc in range(NDC):
        aft = [xnT_done[tt][dc] for tt in range(NT)]
        k.dma("g", H["s_xnT"].ap()[dc * 128:(dc + 1) * 128, :], BIG[:, dc * T:(dc + 1) * T],
              s_xnTs, after=aft)
    ctx["tok_xnT_hbm"] = s_xnTs.tok()
    ctx["tok_xn0_hbm"] = [s_xs[0].tok(), s_xs[1].tok()]
    ctx["xnT_done"] = xnT_done

    if upto == "p0":
        finish_debug(ctx)
        return

    k = ctx["k"]
    SMALL, MID, STAT = ctx["SMALL"], ctx["MID"], ctx["STAT"]
    C_ONES = ctx["C_ONES"]
    ones_ap = CONST[:, C_ONES:C_ONES + 128]
    tok_p0_all = [s_vc.tok(), s_sk.tok()]

    s_p1w = [k.sem("p1_w0"), k.sem("p1_w1")]
    s_p1u = [k.sem("p1_u0"), k.sem("p1_u1")]
    s_p1t = k.sem("p1_tk")

    k.dma("y", MID[0:64, 4096:4353], H["a0_tkT_rev"].ap(), s_p1t)
    k.dma("y", STAT[0:64, 16:32], H["a0_bqT"].ap(), s_p1t)
    tok_p1t = s_p1t.tok()
    tkT_sb = MID[0:64, 4096:4353]

    def wq_src(h):
        return bass.AP(H["a0_wqT"], h * 64, [[DM, 128], [128 * DM, 8], [1, 64]])

    wslot_free = [None, None]
    wload_tok = [None, None]
    ubank_free = [None, None]
    ubslot_free = [None, None]
    prev_qev = []
    prev_umm = None
    dst = H["s_u_in"].ap()
    for h in range(NH):
        par = h % 2
        k.dma("y", MID[:, 3072 + par * 512:3584 + par * 512].rearrange("p (c j) -> p c j", j=64),
              wq_src(h), s_p1w[par], after=(wslot_free[par] or []))
        wload_tok[par] = s_p1w[par].tok()

        qev = []
        for tc in range(4):
            bank = PS[2 + tc % 2]
            aft0 = [wload_tok[par]] + tok_p0_all + [tok_p1t] + prev_qev
            if tc >= 2:
                aft0.append(qev[tc - 2])
            if prev_umm is not None:
                aft0.append(prev_umm)
            mm = None
            for dc in range(NDC):
                mm = k.emit("t", lambda e, b=bank, par=par, dc=dc, tc=tc: e.matmul(
                    b[0:64, :], MID[:, 3072 + par * 512 + dc * 64:3136 + par * 512 + dc * 64],
                    BIG[:, dc * T + tc * 512:dc * T + (tc + 1) * 512],
                    start=(dc == 0), stop=(dc == NDC - 1)),
                    inc=s_tc if dc == NDC - 1 else None,
                    after=aft0 if dc == 0 else ())
            tev = k.emit("v", lambda e, b=bank, tc=tc, h=h: e.tensor_scalar(
                SMALL[0:64, tc * 512:(tc + 1) * 512], b[0:64, :],
                STAT[0:64, 16 + h:17 + h], None, ALU.add),
                inc=s_vc, after=[mm, tok_p1t] + ([prev_umm] if prev_umm else []))
            qev.append(tev)
        wslot_free[par] = [s_tc.tok()]
        prev_qev = list(qev)

        c_dst, s_dst = h % 8, h // 8
        for qt in range(NT):
            par2 = qt % 2
            bank = PS[4 + par2]
            mmu = k.emit("t", lambda e, b=bank, qt=qt: e.matmul(
                b[:, 0:257], SMALL[0:64, qt * 128:(qt + 1) * 128], tkT_sb,
                start=True, stop=True),
                inc=s_tc, after=qev + ([ubank_free[par2]] if ubank_free[par2] else []))
            prev_umm = mmu
            ub = XA[:, par2 * 512:par2 * 512 + 512]
            tv0 = k.emit("v", lambda e, b=bank, ub=ub: e.tensor_copy(ub[:, 0:257], b[:, 0:257]),
                         inc=s_vc, after=[mmu] + ([ubslot_free[par2]] if ubslot_free[par2] else []))
            ubank_free[par2] = tv0
            tv1 = k.emit("v", lambda e, ub=ub: e.tensor_scalar(
                ub[:, 257:385], ones_ap[:, 0:128], ub[:, 0:1], None, ALU.mult),
                inc=s_vc, after=[tv0])
            tv2 = k.emit("v", lambda e, ub=ub: e.tensor_scalar(
                ub[:, 385:512], ones_ap[:, 0:127], ub[:, 256:257], None, ALU.mult),
                inc=s_vc, after=[tv1])
            row0 = qt * 128
            k.dma("g", dst[c_dst, s_dst, row0:row0 + 128, 128:385], ub[:, 0:257],
                  s_p1u[par2], after=[tv0])
            k.dma("g", dst[c_dst, s_dst, row0:row0 + 128, 0:128], ub[:, 257:385],
                  s_p1u[par2], after=[tv1])
            k.dma("g", dst[c_dst, s_dst, row0:row0 + 128, 385:512], ub[:, 385:512],
                  s_p1u[par2], after=[tv2])
            ubslot_free[par2] = s_p1u[par2].tok()
    ctx["tok_p1_ubs"] = [s_p1u[0].tok(), s_p1u[1].tok()]
    k.waits("g", [s_p1u[0].tok(), s_p1u[1].tok()])
    ctx["tok_a2a1"] = k.emit("g", lambda e: e.collective_compute(
        "AllToAll", ALU.bypass, replica_groups=[list(range(NCORE))],
        ins=[H["s_u_in"].ap().opt()], outs=[H["s_u_out"].ap().opt()]),
        inc=ctx["s_cc"])

    if upto == "p1":
        finish_debug(ctx, toks=[ctx["tok_a2a1"]])
        return

    if upto in ("p2", "p2x", "p2a", "all"):
        emit_p2(ctx, upto)
        if upto in ("p2", "p2x", "p2a"):
            finish_debug(ctx, toks=[ctx["tok_ffn0"]])
            return

    emit_p3(ctx)
    if upto == "p3":
        finish_debug(ctx, toks=[ctx["tok_x1_hbm"]])
        return

    emit_p4(ctx, upto)
    if upto in ("p4b", "p4c"):
        finish_debug(ctx)
        return


def finish_debug(ctx, toks=()):
    k = ctx["k"]
    H, RES = ctx["H"], ctx["RES"]
    s_fin = k.sem("fin_st")
    k.waits("g", [ctx["s_vc"].tok(), ctx["s_sk"].tok(), ctx["s_tc"].tok()])
    k.waits("g", [t for t in toks if t is not None])
    for tt in range(NT):
        k.dma("g", H["out"].ap()[tt * 128:(tt + 1) * 128, :], RES[:, tt * DM:(tt + 1) * DM], s_fin)
    k.waits("g", [s_fin.tok()])
    k.waits("y", [s_fin.tok()])


def kernel(**inputs):
    per_core = prep_inputs(inputs)
    nc = build_nc()
    with allow_ici_all_to_all():
        res = run_bass_kernel_spmd(nc, per_core, core_ids=list(range(NCORE)))
    return np.stack([res.results[b]["out"] for b in range(NCORE)], 0)


def emit_p3(ctx):
    k = ctx["k"]
    H, PS = ctx["H"], ctx["PS"]
    XA, RES, BIG, MID, SMALL, CONST, STAT = (ctx["XA"], ctx["RES"], ctx["BIG"], ctx["MID"],
                                             ctx["SMALL"], ctx["CONST"], ctx["STAT"])
    s_tc, s_vc, s_sk = ctx["s_tc"], ctx["s_vc"], ctx["s_sk"]
    C_TRIL, C_TRIU = ctx["C_TRIL"], ctx["C_TRIU"]
    tril = CONST[:, C_TRIL:C_TRIL + 128]
    triu = CONST[:, C_TRIU:C_TRIU + 128]

    XC_SCE = [0, 2048]
    XC_UB = 4096
    XC_UE = 4480
    XC_AT = 4484
    XC_ET = 4996
    XC_TVP = 5252
    XC_TVE = 5380
    XC_W2 = 5508
    XC_STG = 5572
    MC_V = 0
    MC_KT = 1024
    MC_W = [3072, 3072]
    MC_BCB = 1536

    s_p3l = k.sem("p3_ld")
    s_p3w = [k.sem("p3_w0"), k.sem("p3_w1")]
    s_shW = [[k.sem(f"p3_shW{p}{i}") for i in range(2)] for p in range(2)]
    s_shR = [[k.sem(f"p3_shR{p}{i}") for i in range(2)] for p in range(2)]
    s_uo = [k.sem("p3_uo0"), k.sem("p3_uo1")]
    s_w2s = [k.sem("p3_w2s0"), k.sem("p3_w2s1")]
    s_zb = k.sem("p3_zb")
    s_wo = k.sem("p3_wo")
    s_w2r = k.sem("p3_w2r")
    s_x1s = k.sem("p3_x1s")

    tok_big_free = ctx.get("tok_big_free") or ctx["s_tc"].tok()
    tok_ffn0 = ctx.get("tok_ffn0")
    for e in ("t", "v", "s", "g", "y"):
        k.waits(e, ctx.get("tok_res_free") or [])

    for dc in range(NDC):
        k.dma("y", BIG[:, dc * T:(dc + 1) * T], H["s_xnT"].ap()[dc * 128:(dc + 1) * 128, :],
              s_p3l, after=[tok_big_free, ctx["tok_xnT_hbm"]])
    k.dma("y", XA[:, XC_TVP:XC_TVP + 64], H["a0_tvP"].ap()[0:128, :], s_p3l)
    k.dma("y", XA[:, XC_TVP + 64:XC_TVP + 128], H["a0_tvP"].ap()[128:256, :], s_p3l)
    k.dma("y", XA[:, XC_TVE:XC_TVE + 64],
          bass.AP(H["a0_tv_edge"], 0, [[0, 128], [1, 64]]), s_p3l)
    k.dma("y", XA[:, XC_TVE + 64:XC_TVE + 128],
          bass.AP(H["a0_tv_edge"], 64, [[0, 128], [1, 64]]), s_p3l)
    tok_v_now = ctx["s_vc"].tok()
    k.dma("y", STAT[0:64, 16:32], H["a0_bqT"].ap(), s_p3l, after=[tok_v_now])
    k.dma("y", STAT[0:64, 32:48], H["a0_bkT"].ap(), s_p3l)
    tok_p3l = s_p3l.tok()

    zt = k.emit("v", lambda e: e.memset(XA[:, XC_SCE[0]:XC_SCE[0] + 511], 0.0), inc=s_vc,
                after=ctx.get("tok_p1_ubs") or [])
    ZCH = 128 * 511
    for par in range(2):
        for i in range(16):
            k.dma("g", bass.AP(H["s_eshear"], par * (T * 511 + 8) + i * ZCH, [[1, ZCH]]),
                  XA[:, XC_SCE[0]:XC_SCE[0] + 511], s_zb, after=[zt])
    tok_zb = s_zb.tok()
    p0toks = list(ctx["tok_xn0_hbm"]) + (ctx.get("tok_p1_ubs") or [])
    sce_free = [[tok_zb] + p0toks, [tok_zb] + p0toks]

    def wslice_src(hsrc, h):
        return bass.AP(H[hsrc], h * 64, [[DM, 128], [128 * DM, 8], [1, 64]])

    uo_base = (H["s_u_out"], lambda h2: ((h2 // 2) * 2 + (h2 % 2)) * T * 512)
    esh_base = lambda par: par * (T * 511 + 8)

    wslot_free = None
    wload_tok = None
    et_free = [None, None]
    at_free = None
    ub_free = []
    w2psum_free = None
    w1psum_free = None
    w2scr_free = None
    prev_qkv_ev = []
    prev_w1mm = None
    tok_w2_stores = []
    for h2 in range(NH):
        par = h2 % 2
        wb = MC_W[par]
        for ci, hsrc in enumerate(("a0_wqT", "a0_wkT", "a0_wvT")):
            k.dma("y", MID[:, wb + ci * 512:wb + (ci + 1) * 512].rearrange("p (c j) -> p c j", j=64),
                  wslice_src(hsrc, h2), s_p3w[par], after=(wslot_free or []))
        wload_tok = s_p3w[par].tok()

        qkv_ev = []
        for which in range(2):
            for tc in range(4):
                bank = PS[which * 2 + tc % 2]
                aft0 = [wload_tok, tok_p3l] + prev_qkv_ev + ([qkv_ev[-2]] if tc >= 2 else [])
                mm = None
                for dc in range(NDC):
                    mm = k.emit("t", lambda e, b=bank, wb=wb, w=which, dc=dc, tc=tc: e.matmul(
                        b[0:64, :], MID[:, wb + w * 512 + dc * 64:wb + w * 512 + dc * 64 + 64],
                        BIG[:, dc * T + tc * 512:dc * T + (tc + 1) * 512],
                        start=(dc == 0), stop=(dc == NDC - 1)),
                        inc=s_tc if dc == NDC - 1 else None,
                        after=aft0 if dc == 0 else ())
                if which == 0:
                    dstap = SMALL[0:64, tc * 512:(tc + 1) * 512]
                    bcol = STAT[0:64, 16 + h2:17 + h2]
                else:
                    dstap = MID[0:64, MC_KT + tc * 512:MC_KT + (tc + 1) * 512]
                    bcol = STAT[0:64, 32 + h2:33 + h2]
                tev = k.emit("v", lambda e, b=bank, d=dstap, bc=bcol: e.tensor_scalar(
                    d, b[0:64, :], bc, None, ALU.add), inc=s_vc, after=[mm, tok_p3l])
                qkv_ev.append(tev)
        for tt in range(NT):
            bank = PS[4 + tt % 2]
            aft0 = [wload_tok] + prev_qkv_ev + ([qkv_ev[-2]] if tt >= 2 else [])
            mm = None
            for dc in range(NDC):
                mm = k.emit("t", lambda e, b=bank, wb=wb, dc=dc, tt=tt: e.matmul(
                    b[:, 0:64], BIG[:, dc * T + tt * 128:dc * T + (tt + 1) * 128],
                    MID[:, wb + 1024 + dc * 64:wb + 1024 + dc * 64 + 64],
                    start=(dc == 0), stop=(dc == NDC - 1)),
                    inc=s_tc if dc == NDC - 1 else None,
                    after=aft0 if dc == 0 else ())
            tev = k.emit("s", lambda e, b=bank, tt=tt: e.copy(
                MID[:, MC_V + tt * 64:MC_V + (tt + 1) * 64], b[:, 0:64]),
                inc=s_sk, after=[mm])
            qkv_ev.append(tev)
        wslot_free = [s_tc.tok()]
        prev_qkv_ev = list(qkv_ev)

        for qt in range(NT):
            q0 = qt * 128
            sp = qt % 2
            sce = XA[:, XC_SCE[sp]:XC_SCE[sp] + 2048]
            smm = []
            for kc in range(4):
                aft = qkv_ev + sce_free[sp]
                smm.append(k.emit("t", lambda e, b=PS[kc], kc=kc, q0=q0: e.matmul(
                    b[:, :], SMALL[0:64, q0:q0 + 128],
                    MID[0:64, MC_KT + kc * 512:MC_KT + (kc + 1) * 512],
                    start=True, stop=True), inc=s_tc, after=aft))
            sev = []
            for kc in range(4):
                if kc < 2:
                    sev.append(k.emit("v", lambda e, b=PS[kc], kc=kc, sce=sce: e.tensor_copy(
                        sce[:, kc * 512:(kc + 1) * 512], b[:, :]), inc=s_vc,
                        after=[smm[kc]] + sce_free[sp]))
                else:
                    sev.append(k.emit("s", lambda e, b=PS[kc], kc=kc, sce=sce: e.copy(
                        sce[:, kc * 512:(kc + 1) * 512], b[:, :]), inc=s_sk,
                        after=[smm[kc]] + sce_free[sp]))
            ubase, boff = uo_base
            bis = [bi for bi in (0, 1, 2) if 0 <= q0 + (bi - 1) * 128 < T]
            tub = []
            for bi in bis:
                kk0 = q0 + (bi - 1) * 128
                src = bass.AP(ubase, boff(h2) + q0 * 511 + kk0 + 256, [[511, 128], [1, 128]])
                tub.append(k.dma("y", XA[:, XC_UB + bi * 128:XC_UB + (bi + 1) * 128], src,
                                 s_uo[sp], after=[ctx["tok_a2a1"]] + ub_free))
            tue = k.dma("y", XA[:, XC_UE:XC_UE + 2],
                        bass.AP(ubase, boff(h2) + q0 * 512 + 128, [[512, 128], [256, 2]]),
                        s_uo[sp], after=[ctx["tok_a2a1"]] + ub_free,
                        allow_slow_non_contiguous=True)
            tok_uo = s_uo[sp].tok()
            tadd = []
            for bi in bis:
                kk0 = q0 + (bi - 1) * 128
                tadd.append(k.emit("v", lambda e, sce=sce, bi=bi, kk0=kk0: e.tensor_tensor(
                    sce[:, kk0:kk0 + 128], sce[:, kk0:kk0 + 128],
                    XA[:, XC_UB + bi * 128:XC_UB + (bi + 1) * 128], ALU.add),
                    inc=s_vc, after=sev + [tok_uo]))
            ledge = None
            redge = None
            if q0 - 128 > 0:
                ledge = k.emit("s", lambda e, sce=sce, w=q0 - 128: e.add(
                    sce[:, 0:w], sce[:, 0:w], XA[:, XC_UE:XC_UE + 1]),
                    inc=s_sk, after=sev + [tok_uo])
            if q0 + 256 < T:
                redge = k.emit("s", lambda e, sce=sce, a=q0 + 256: e.add(
                    sce[:, a:T], sce[:, a:T], XA[:, XC_UE + 1:XC_UE + 2]),
                    inc=s_sk, after=sev + [tok_uo])
            tok_s2done = tadd + [t for t in (ledge, redge) if t]

            tmax = k.emit("v", lambda e, sce=sce: e.reduce_max(STAT[:, 8:9], sce, axis=AX),
                          inc=s_vc, after=tok_s2done)
            tmn = k.emit("v", lambda e: e.tensor_scalar(STAT[:, 9:10], STAT[:, 8:9], -0.125,
                                                        None, ALU.mult), inc=s_vc, after=[tmax])
            texp = k.emit("s", lambda e, sce=sce: e.activation(
                sce, sce, AF.Exp, bias=STAT[:, 9:10], scale=0.125,
                accum_out=STAT[:, 10:11]), inc=s_sk, after=[tmn] + tok_s2done)
            trs = k.emit("v", lambda e: e.reciprocal(STAT[:, 11:12], STAT[:, 10:11]),
                         inc=s_vc, after=[texp])

            wsem = s_shW[par][qt % 2]
            rsem = s_shR[par][qt % 2]
            for bi in bis:
                kk0 = q0 + (bi - 1) * 128
                dst = bass.AP(H["s_eshear"], esh_base(par) + q0 * 510 + kk0 + 255,
                              [[510, 128], [1, 128]])
                k.dma("g", dst, sce[:, kk0:kk0 + 128], wsem,
                      after=[texp, tok_zb, rsem.tok()])
            tok_shw = wsem.tok()
            tatr = k.dma("y", XA[:, XC_AT + 256:XC_AT + 511],
                         bass.AP(H["s_eshear"], esh_base(par) + q0 * 511 + 128, [[511, 128], [1, 255]]),
                         rsem, after=([tok_shw, at_free] if at_free else [tok_shw]))
            ta0 = k.emit("t", lambda e: e.transpose(
                PS[4][:, 0:128], XA[:, XC_AT + 256:XC_AT + 384],
                CONST[:, ctx["C_IDENT"]:ctx["C_IDENT"] + 128]), inc=s_tc,
                after=[tatr] + ([et_free[0]] if et_free[0] else []))
            ta1 = k.emit("t", lambda e: e.transpose(
                PS[5][0:127, 0:128], XA[:, XC_AT + 384:XC_AT + 511],
                CONST[:, ctx["C_IDENT"]:ctx["C_IDENT"] + 128]), inc=s_tc,
                after=[tatr] + ([et_free[1]] if et_free[1] else []))
            tat0 = k.emit("v", lambda e: e.tensor_copy(XA[:, XC_AT:XC_AT + 128], PS[4][:, 0:128]),
                          inc=s_vc, after=[ta0])
            tat1 = k.emit("s", lambda e: e.copy(XA[0:127, XC_AT + 128:XC_AT + 256], PS[5][0:127, 0:128]),
                          inc=s_sk, after=[ta1])
            et_free[0] = tat0
            et_free[1] = tat1
            lw = q0 - 128
            tLR = []
            tl1 = tl2 = tr1 = tr2 = None
            if qt >= 1:
                if lw > 0:
                    tl1 = k.emit("v", lambda e, sce=sce, lw=lw: e.reduce_sum(
                        STAT[:, 12:13], sce[:, 0:lw], axis=AX), inc=s_vc, after=[texp])
                tl2a = k.emit("v", lambda e, sce=sce, lw=max(lw, 0): e.tensor_tensor(
                    XA[:, XC_UB:XC_UB + 128], sce[:, lw:lw + 128], tril, ALU.mult),
                    inc=s_vc, after=[texp])
                tl2 = k.emit("v", lambda e: e.reduce_sum(STAT[:, 13:14],
                                                         XA[:, XC_UB:XC_UB + 128], axis=AX),
                             inc=s_vc, after=[tl2a])
                if tl1 is not None:
                    tLR.append(k.emit("v", lambda e: e.tensor_tensor(
                        STAT[:, 12:13], STAT[:, 12:13], STAT[:, 13:14], ALU.add),
                        inc=s_vc, after=[tl1, tl2]))
                else:
                    tLR.append(k.emit("v", lambda e: e.tensor_copy(STAT[:, 12:13], STAT[:, 13:14]),
                                      inc=s_vc, after=[tl2]))
            if qt <= NT - 2:
                rw0 = q0 + 256
                tr2a = k.emit("v", lambda e, sce=sce, a=q0 + 128: e.tensor_tensor(
                    XA[:, XC_UB + 128:XC_UB + 256], sce[:, a:a + 128], triu, ALU.mult),
                    inc=s_vc, after=[texp])
                tr2 = k.emit("v", lambda e: e.reduce_sum(STAT[:, 15:16],
                                                         XA[:, XC_UB + 128:XC_UB + 256], axis=AX),
                             inc=s_vc, after=[tr2a])
                if rw0 < T:
                    tr1 = k.emit("v", lambda e, sce=sce, a=rw0: e.reduce_sum(
                        STAT[:, 14:15], sce[:, a:T], axis=AX), inc=s_vc, after=[texp])
                    tLR.append(k.emit("v", lambda e: e.tensor_tensor(
                        STAT[:, 14:15], STAT[:, 14:15], STAT[:, 15:16], ALU.add),
                        inc=s_vc, after=[tr1, tr2]))
                else:
                    tLR.append(k.emit("v", lambda e: e.tensor_copy(STAT[:, 14:15], STAT[:, 15:16]),
                                      inc=s_vc, after=[tr2]))

            m1 = k.emit("t", lambda e: e.matmul(PS[7][:, 0:64], XA[:, XC_AT:XC_AT + 128],
                                                XA[:, XC_TVP:XC_TVP + 64], start=True, stop=False),
                        inc=None, after=[tat0, tat1, tok_p3l] + [x for x in (prev_w1mm, w2psum_free) if x])
            m2 = k.emit("t", lambda e: e.matmul(PS[7][:, 0:64], XA[0:127, XC_AT + 128:XC_AT + 256],
                                                XA[0:127, XC_TVP + 64:XC_TVP + 128],
                                                start=False, stop=True), inc=s_tc)
            at_free = m2
            aftw2 = [m2, trs] + tLR + ([w2scr_free] if w2scr_free else [])
            if qt >= 1:
                w2a = k.emit("v", lambda e: e.scalar_tensor_tensor(
                    XA[:, XC_W2:XC_W2 + 64], XA[:, XC_TVE:XC_TVE + 64], STAT[:, 12:13],
                    PS[7][:, 0:64], ALU.mult, ALU.add), inc=s_vc, after=aftw2)
            else:
                w2a = k.emit("v", lambda e: e.tensor_copy(XA[:, XC_W2:XC_W2 + 64], PS[7][:, 0:64]),
                             inc=s_vc, after=aftw2)
            w2psum_free = w2a
            if qt <= NT - 2:
                w2b = k.emit("v", lambda e: e.scalar_tensor_tensor(
                    XA[:, XC_W2:XC_W2 + 64], XA[:, XC_TVE + 64:XC_TVE + 128], STAT[:, 14:15],
                    XA[:, XC_W2:XC_W2 + 64], ALU.mult, ALU.add), inc=s_vc, after=[w2a])
            else:
                w2b = w2a
            w2c = k.emit("v", lambda e: e.tensor_scalar(
                XA[:, XC_W2:XC_W2 + 64], XA[:, XC_W2:XC_W2 + 64], STAT[:, 11:12], None, ALU.mult),
                inc=s_vc, after=[w2b, trs])
            ub_free = [s_vc.tok(), s_sk.tok()]
            k.dma("g", H["s_w2_in"].ap()[h2 % 8, h2 // 8, q0:q0 + 128, :],
                  XA[:, XC_W2:XC_W2 + 64], s_w2s[sp], after=[w2c])
            w2scr_free = s_w2s[sp].tok()
            tok_w2_stores = [s_w2s[0].tok(), s_w2s[1].tok()]

            wmm = None
            for kc in range(NT):
                ep = kc % 2
                bank = PS[4 + ep]
                ttr = k.emit("t", lambda e, sce=sce, kc=kc, b=bank: e.transpose(
                    b[:, 0:128], sce[:, kc * 128:(kc + 1) * 128],
                    CONST[:, ctx["C_IDENT"]:ctx["C_IDENT"] + 128]),
                    inc=s_tc, after=[texp] + ([et_free[ep]] if et_free[ep] else []))
                tev = k.emit("s", lambda e, b=bank, ep=ep: e.copy(
                    XA[:, XC_ET + ep * 128:XC_ET + (ep + 1) * 128], b[:, 0:128]),
                    inc=s_sk, after=[ttr] + ([wmm] if wmm and kc >= 2 else []))
                wmm = k.emit("t", lambda e, ep=ep, kc=kc: e.matmul(
                    PS[6][:, 0:64], XA[:, XC_ET + ep * 128:XC_ET + (ep + 1) * 128],
                    MID[:, MC_V + kc * 64:MC_V + (kc + 1) * 64],
                    start=(kc == 0), stop=(kc == NT - 1)),
                    inc=s_tc if kc == NT - 1 else None,
                    after=[tev] + ([w1psum_free] if (kc == 0 and w1psum_free) else []))
                et_free[ep] = wmm if wmm else None
            prev_w1mm = wmm
            tw1 = k.emit("v", lambda e, qt=qt, h2=h2: e.tensor_scalar(
                RES[:, qt * DM + h2 * 64:qt * DM + h2 * 64 + 64], PS[6][:, 0:64],
                STAT[:, 11:12], None, ALU.mult), inc=s_vc, after=[wmm, trs])
            w1psum_free = tw1
            sce_free[sp] = [s_tc.tok(), wsem.tok()]
    ctx["tok_w1_all"] = s_vc.tok()

    k.waits("g", tok_w2_stores)
    tok_a2a2 = k.emit("g", lambda e: e.collective_compute(
        "AllToAll", ALU.bypass, replica_groups=[list(range(NCORE))],
        ins=[H["s_w2_in"].ap().opt()], outs=[H["s_w2_out"].ap().opt()]),
        inc=ctx["s_cc"])
    ctx["tok_a2a2"] = tok_a2a2

    for dc in range(NDC):
        k.dma("y", BIG[:, dc * 1024:(dc + 1) * 1024],
              H["a0_woT"].ap()[dc * 128:(dc + 1) * 128, :], s_wo,
              after=[s_tc.tok()])
    k.dma("y", MID[:, 3072:4096], bass.AP(H["a0_bv_row"], 0, [[0, 128], [1, 1024]]), s_wo)
    k.dma("y", MID[:, 1024:2048], bass.AP(H["a0_bo"], 0, [[0, 128], [1, 1024]]), s_wo)
    tok_wo_w = s_wo.tok()

    s_xl = k.sem("p3_xl")
    stg_free = None
    opT_free = None
    x1_toks = []
    for qt in range(NT):
        q0 = qt * 128
        aft = [tok_a2a2] + ([stg_free] if stg_free else [])
        rts = []
        for h2 in range(NH):
            rts.append(k.dma("y", XA[:, XC_STG + h2 * 64:XC_STG + (h2 + 1) * 64],
                             H["s_w2_out"].ap()[h2 // 2, h2 % 2, q0:q0 + 128, :], s_w2r,
                             after=aft))
        tok_stg = s_w2r.tok()
        a1 = k.emit("v", lambda e, qt=qt: e.tensor_tensor(
            RES[:, qt * DM:(qt + 1) * DM], RES[:, qt * DM:(qt + 1) * DM],
            XA[:, XC_STG:XC_STG + 1024], ALU.add), inc=s_vc,
            after=[tok_stg, ctx["tok_w1_all"]])
        a2 = k.emit("v", lambda e, qt=qt: e.tensor_tensor(
            RES[:, qt * DM:(qt + 1) * DM], RES[:, qt * DM:(qt + 1) * DM],
            MID[:, 3072:4096], ALU.add), inc=s_vc, after=[a1, tok_wo_w])
        stg_free = a1
        opT = XA[:, 0:1024]
        tevs = []
        for dc in range(NDC):
            ep = dc % 2
            bank = PS[4 + ep]
            ttr = k.emit("t", lambda e, qt=qt, dc=dc, b=bank: e.transpose(
                b[:, 0:128], RES[:, qt * DM + dc * 128:qt * DM + (dc + 1) * 128],
                CONST[:, ctx["C_IDENT"]:ctx["C_IDENT"] + 128]),
                inc=s_tc, after=[a2] + ([opT_free] if opT_free else [])
                + ([tevs[dc - 2]] if dc >= 2 else []))
            tev = k.emit("s", lambda e, b=bank, dc=dc: e.copy(
                opT[:, dc * 128:(dc + 1) * 128], b[:, 0:128]), inc=s_sk, after=[ttr])
            tevs.append(tev)
        shw_toks = [s_shW[p][i].tok() for p in range(2) for i in range(2)]
        lx = k.dma("y", XA[:, 1024:2048], H["x0"].ap()[q0:q0 + 128, :], s_xl,
                   after=([stg_free] if stg_free else []) + shw_toks)
        lf = k.dma("y", XA[:, 2048 + 0:3072], H["s_ffn0"].ap()[q0:q0 + 128, :], s_xl,
                   after=([tok_ffn0] if tok_ffn0 else []))
        tok_xl = s_xl.tok()
        for eo in range(2):
            mm = None
            for dc in range(NDC):
                mm = k.emit("t", lambda e, eo=eo, dc=dc: e.matmul(
                    PS[eo][:, :], opT[:, dc * 128:(dc + 1) * 128],
                    BIG[:, dc * 1024 + eo * 512:dc * 1024 + (eo + 1) * 512],
                    start=(dc == 0), stop=(dc == NDC - 1)),
                    inc=s_tc if dc == NDC - 1 else None,
                    after=(tevs + [tok_wo_w]) if dc == 0 else ())
            c1 = k.emit("v", lambda e, eo=eo, qt=qt: e.tensor_tensor(
                RES[:, qt * DM + eo * 512:qt * DM + (eo + 1) * 512], PS[eo][:, :],
                XA[:, 1024 + eo * 512:1024 + (eo + 1) * 512], ALU.add),
                inc=s_vc, after=[mm, tok_xl])
            c2 = k.emit("v", lambda e, eo=eo, qt=qt: e.tensor_tensor(
                RES[:, qt * DM + eo * 512:qt * DM + (eo + 1) * 512],
                RES[:, qt * DM + eo * 512:qt * DM + (eo + 1) * 512],
                XA[:, 2048 + eo * 512:2048 + (eo + 1) * 512], ALU.add),
                inc=s_vc, after=[c1])
            c3 = k.emit("v", lambda e, eo=eo, qt=qt: e.tensor_tensor(
                RES[:, qt * DM + eo * 512:qt * DM + (eo + 1) * 512],
                RES[:, qt * DM + eo * 512:qt * DM + (eo + 1) * 512],
                MID[:, 1024 + eo * 512:1024 + (eo + 1) * 512], ALU.add),
                inc=s_vc, after=[c2, tok_wo_w])
            x1_toks.append(c3)
        opT_free = s_tc.tok()
        k.dma("g", H["s_x1"].ap()[q0:q0 + 128, :], RES[:, qt * DM:(qt + 1) * DM],
              s_x1s, after=x1_toks[-2:])
    ctx["tok_x1_hbm"] = s_x1s.tok()
    ctx["tok_x1_res"] = x1_toks


def emit_p2(ctx, upto):
    k = ctx["k"]
    H, PS = ctx["H"], ctx["PS"]
    XA, RES, BIG, MID, SMALL, CONST, STAT = (ctx["XA"], ctx["RES"], ctx["BIG"], ctx["MID"],
                                             ctx["SMALL"], ctx["CONST"], ctx["STAT"])
    s_tc, s_vc, s_sk = ctx["s_tc"], ctx["s_vc"], ctx["s_sk"]
    ident_ap = CONST[:, ctx["C_IDENT"]:ctx["C_IDENT"] + 128]
    C_LAM = ctx["C_LAM"]

    def barrier(dma_toks=()):
        toks = [s_tc.tok(), s_vc.tok(), s_sk.tok()] + [t for t in dma_toks if t]
        for e in ("t", "v", "s", "g", "y"):
            k.waits(e, toks)
        return toks

    ent = ([s_tc.tok(), s_vc.tok(), s_sk.tok(), ctx["tok_xnT_hbm"]]
           + list(ctx["tok_xn0_hbm"]) + (ctx.get("tok_p1_ubs") or []))
    for e in ("t", "v", "s", "g", "y"):
        k.waits(e, ent)

    s_init = k.sem("p2_init")
    s_cl = k.sem("p2_cl")
    s_wq = [k.sem("p2_wq0"), k.sem("p2_wq1")]
    s_rt = k.sem("p2_rt")
    s_w1 = [k.sem("p2_w10"), k.sem("p2_w11")]
    s_w2w = [k.sem("p2_w2w0"), k.sem("p2_w2w1")]
    s_wo = [k.sem("p2_wo0"), k.sem("p2_wo1")]
    s_xs = k.sem("p2_xs")
    s_bd = k.sem("p2_bd")
    s_fs = k.sem("p2_fs")

    for tt in range(NT):
        k.dma("y", RES[:, tt * DM:(tt + 1) * DM], H["s_xn0"].ap()[tt * 128:(tt + 1) * 128, :],
              s_init, after=ctx["tok_xn0_hbm"])
    tok_init = s_init.tok()
    for e in ("t", "v", "s"):
        k.waits(e, [tok_init])

    def bcast_row(dst, hsrc, row, width, sem, after=()):
        return k.dma("y", dst, bass.AP(H[hsrc], row * width, [[0, 128], [1, width]]),
                     sem, after=after)

    import numpy as _np
    lam_full, lam_part = {}, {}
    for w in range(8):
        for j in range(2):
            for kc in range(4):
                q_abs = w * 256 + j * 128 + _np.arange(128)[None, :]
                k_abs = (w - 1) * 256 + kc * 128 + _np.arange(128)[:, None]
                vv_ = (k_abs <= q_abs) & (k_abs >= q_abs - 256) & (k_abs >= 0)
                lam_full[(w, j, kc)] = bool(vv_.all())
                lam_part[(w, j, kc)] = bool(vv_.any()) and not vv_.all()

    qst, kst, vbf, awin = 1024, 3072, 9216, 13312

    for li in range(1 if upto in ('p2x', 'p2a') else 6):
        tok_bar = barrier()
        bcast_row(MID[:, 256:320], "f_qs", li, DH, s_cl, after=tok_bar)
        bcast_row(MID[:, 320:384], "f_ks", li, DH, s_cl)
        bcast_row(MID[:, 384:1408], "f_lng", li, DM, s_cl)
        bcast_row(MID[:, 1408:2432], "f_lnb", li, DM, s_cl)
        bcast_row(MID[:, 2476:3500], "f_b2", li, DM, s_cl)
        k.dma("y", MID[:, 2432:2454], H["f_b1aT"].ap()[li], s_cl)
        k.dma("y", MID[:, 2454:2476], H["f_b1gT"].ap()[li], s_cl)
        tok_cl = s_cl.tok()

        xT = XA[:, 0:4104]
        wq_free = [None, None]
        wo_free = [None, None]
        tok_xT_prev = None

        for w in range(8):
            tiles = (2 * w, 2 * w + 1)
            if w % 2 == 0:
                c = w // 2
                tok_bar = barrier()
                if c == 0:
                    tbd = k.emit("v", lambda e: e.memset(
                        bass.AP(XA, 0, [[XA.shape[1], 128], [513, 8], [1, 1]]), 0.0),
                        inc=s_vc, after=tok_bar + [tok_cl])
                else:
                    tbd = k.emit("v", lambda e: e.tensor_copy(
                        bass.AP(XA, 0, [[XA.shape[1], 128], [513, 8], [1, 1]]),
                        bass.AP(XA, 512, [[XA.shape[1], 128], [513, 8], [1, 1]])),
                        inc=s_vc, after=tok_bar)
                trps_free = [None, None]
                for lt in range(4):
                    tt = c * 4 + lt
                    for dc in range(NDC):
                        bp = dc % 2
                        bank = PS[2 + bp]
                        ttr = k.emit("t", lambda e, tt=tt, dc=dc, b=bank: e.transpose(
                            b[:, 0:128], RES[:, tt * DM + dc * 128:tt * DM + (dc + 1) * 128],
                            ident_ap), inc=s_tc,
                            after=[tbd] + ([trps_free[bp]] if trps_free[bp] else []))
                        dst = dc * 513 + 1 + lt * 128
                        eng, ch = ("v", s_vc) if dc % 2 == 0 else ("s", s_sk)
                        fcp = (lambda e, b=bank, dst=dst: e.tensor_copy(
                            xT[:, dst:dst + 128], b[:, 0:128])) if eng == "v" else \
                              (lambda e, b=bank, dst=dst: e.copy(xT[:, dst:dst + 128], b[:, 0:128]))
                        tev = k.emit(eng, fcp, inc=ch, after=[ttr, tbd])
                        trps_free[bp] = tev

            tok_bar = barrier()
            qkps_free = [None, None]
            for oc in range(6):
                for dc in range(NDC):
                    par = (oc * NDC + dc) % 2
                    k.dma("y", BIG[:, par * 512:(par + 1) * 512],
                          bass.AP(H["f_wqkv"], li * DM * 3 * DM + dc * 128 * 3 * DM + oc * 512,
                                  [[3 * DM, 128], [1, 512]]),
                          s_wq[par], after=([wq_free[par]] if wq_free[par] else []) + [tok_cl] + tok_bar)
                    tok_w = s_wq[par].tok()
                    for lt in range(2):
                        tt = tiles[lt]
                        bank = PS[lt]
                        off = 0 if dc < 4 else -1
                        xoff = dc * 513 + 1 + (tt % 4) * 128 + off
                        aft = [tok_w] + (([qkps_free[lt]] if qkps_free[lt] else []) if dc == 0 else [])
                        k.emit("t", lambda e, b=bank, xoff=xoff, par=par, dc=dc: e.matmul(
                            b[:, :], xT[:, xoff:xoff + 128],
                            BIG[:, par * 512:(par + 1) * 512],
                            start=(dc == 0), stop=(dc == NDC - 1)),
                            inc=s_tc if lt == 1 else None, after=aft)
                    wq_free[par] = s_tc.tok()
                tok_oc = s_tc.tok()
                for lt in range(2):
                    tt = tiles[lt]
                    if oc < 2:
                        dstap = BIG[:, qst + lt * 1024 + oc * 512:qst + lt * 1024 + (oc + 1) * 512]
                    elif oc < 4:
                        dstap = BIG[:, kst + lt * 1024 + (oc - 2) * 512:kst + lt * 1024 + (oc - 1) * 512]
                    else:
                        dstap = BIG[:, vbf + (w % 2) * 2048 + lt * 1024 + (oc - 4) * 512:
                                    vbf + (w % 2) * 2048 + lt * 1024 + (oc - 3) * 512]
                    eng, ch = ("v", s_vc) if lt == 0 else ("s", s_sk)
                    fcp = (lambda e, b=PS[lt], d=dstap: e.tensor_copy(d, b[:, :])) if eng == "v" else \
                          (lambda e, b=PS[lt], d=dstap: e.copy(d, b[:, :]))
                    tev = k.emit(eng, fcp, inc=ch, after=[tok_oc])
                    qkps_free[lt] = tev

            tok_bar = barrier()
            rot_free = []
            trps_free = [None, None]
            for lt in range(2):
                tt = tiles[lt]
                for ri, rname in enumerate(("rot_cs", "rot_ss", "rot_ci", "rot_si")):
                    k.dma("y", MID[:, ri * 64:(ri + 1) * 64],
                          H[rname].ap()[tt * 128:(tt + 1) * 128, :], s_rt,
                          after=rot_free + [tok_cl] + tok_bar)
                tok_rt = s_rt.tok()
                for which in range(2):
                    st0 = (qst if which == 0 else kst) + lt * 1024
                    stg = BIG[:, st0:st0 + 1024]
                    scr = XA[:, 5128:6152]
                    scrB = XA[:, 4104:5128]
                    sqc = STAT[:, 8:24]
                    def v3(buf, off0, hstep, dstep, dcount):
                        return bass.AP(buf.tensor, buf.offset + off0,
                                       [[buf.tensor.shape[1], 128], [hstep, 16], [dstep, dcount]])
                    t1 = k.emit("v", lambda e, stg=stg, scr=scr: e.tensor_tensor(
                        scr, stg, stg, ALU.mult), inc=s_vc, after=tok_bar)
                    t2 = k.emit("v", lambda e, scr=scr, sqc=sqc: e.reduce_sum(
                        sqc, scr.rearrange("p (h d) -> p h d", d=64), axis=AX),
                        inc=s_vc, after=[t1])
                    t3 = k.emit("s", lambda e, sqc=sqc: e.activation(
                        sqc, sqc, AF.Sqrt), inc=s_sk, after=[t2])
                    t4 = k.emit("v", lambda e, sqc=sqc: e.tensor_scalar(
                        sqc, sqc, 1e-12, None, ALU.max), inc=s_vc, after=[t3])
                    t5 = k.emit("v", lambda e, sqc=sqc: e.reciprocal(sqc, sqc),
                                inc=s_vc, after=[t4])
                    t6 = k.emit("v", lambda e, stg=stg, sqc=sqc: e.tensor_tensor(
                        stg.rearrange("p (h d) -> p d h", d=64),
                        stg.rearrange("p (h d) -> p d h", d=64),
                        sqc.unsqueeze(1).broadcast_to([128, 64, 16]), ALU.mult),
                        inc=s_vc, after=[t5])
                    qsc = MID[:, 256 + which * 64:320 + which * 64]
                    t7 = k.emit("v", lambda e, stg=stg, qsc=qsc: e.tensor_tensor(
                        stg.rearrange("p (h d) -> p h d", d=64),
                        stg.rearrange("p (h d) -> p h d", d=64),
                        qsc.unsqueeze(1).broadcast_to([128, 16, 64]), ALU.mult),
                        inc=s_vc, after=[t6, tok_rt])
                    cosc = MID[:, (0 if which == 0 else 128):(64 if which == 0 else 192)]
                    sinc = MID[:, (64 if which == 0 else 192):(128 if which == 0 else 256)]
                    tA = k.emit("v", lambda e, stg=stg, scr=scr, cosc=cosc: e.tensor_tensor(
                        scr.rearrange("p (h d) -> p h d", d=64),
                        stg.rearrange("p (h d) -> p h d", d=64),
                        cosc.unsqueeze(1).broadcast_to([128, 16, 64]), ALU.mult),
                        inc=s_vc, after=[t7])
                    tB0 = k.emit("v", lambda e, stg=stg, scrB=scrB, sinc=sinc, v3=v3: e.tensor_tensor(
                        v3(scrB, 0, 64, 1, 32), v3(stg, 32, 64, 1, 32),
                        v3(sinc, 0, 0, 1, 32), ALU.mult), inc=s_vc, after=[t7])
                    tB1 = k.emit("v", lambda e, stg=stg, scrB=scrB, sinc=sinc, v3=v3: e.tensor_tensor(
                        v3(scrB, 32, 64, 1, 32), v3(stg, 0, 64, 1, 32),
                        v3(sinc, 32, 0, 1, 32), ALU.mult), inc=s_vc, after=[t7])
                    tQ0 = k.emit("v", lambda e, stg=stg, scr=scr, scrB=scrB, v3=v3: e.tensor_tensor(
                        v3(stg, 0, 64, 1, 32), v3(scr, 0, 64, 1, 32),
                        v3(scrB, 0, 64, 1, 32), ALU.subtract), inc=s_vc, after=[tA, tB0, tB1])
                    tQ1 = k.emit("v", lambda e, stg=stg, scr=scr, scrB=scrB, v3=v3: e.tensor_tensor(
                        v3(stg, 32, 64, 1, 32), v3(scr, 32, 64, 1, 32),
                        v3(scrB, 32, 64, 1, 32), ALU.add), inc=s_vc, after=[tA, tB0, tB1])
                    rot_done = [tQ0, tQ1]
                    for pp in range(8):
                        bp = pp % 2
                        bank = PS[2 + bp]
                        ttr = k.emit("t", lambda e, stg=stg, pp=pp, b=bank: e.transpose(
                            b[:, 0:128], stg[:, pp * 128:(pp + 1) * 128], ident_ap),
                            inc=s_tc, after=rot_done + ([trps_free[bp]] if trps_free[bp] else []))
                        if which == 0:
                            dst = SMALL[:, pp * 256 + lt * 128:pp * 256 + lt * 128 + 128]
                        else:
                            dst = SMALL[:, 2048 + pp * 512 + (w % 2) * 256 + lt * 128:
                                        2048 + pp * 512 + (w % 2) * 256 + lt * 128 + 128]
                        eng, ch = ("v", s_vc) if pp % 2 == 0 else ("s", s_sk)
                        fcp = (lambda e, b=bank, d=dst: e.tensor_copy(d, b[:, 0:128])) if eng == "v" else \
                              (lambda e, b=bank, d=dst: e.copy(d, b[:, 0:128]))
                        tev = k.emit(eng, fcp, inc=ch, after=[ttr])
                        trps_free[bp] = tev
                rot_free = [s_vc.tok(), s_sk.tok(), s_tc.tok()]

            tok_bar = barrier()
            kcs = [kc for kc in range(4) if any(lam_full[(w, j, kc)] or lam_part[(w, j, kc)]
                                                for j in range(2))]
            ebuf = XA[:, 4104:5128]
            win_ps_free = {}
            ebuf_free = []
            for h in range(NH):
                pp, hp = h // 2, h % 2
                rows = slice(hp * 64, hp * 64 + 64)
                sc_mm = {}
                for kc in kcs:
                    wslot = ((w - 1 + kc // 2) % 2)
                    kcol = 2048 + pp * 512 + wslot * 256 + (kc % 2) * 128
                    qcol = pp * 256
                    bank = PS[4 + kc]
                    sc_mm[kc] = k.emit("t", lambda e, b=bank, kcol=kcol, qcol=qcol, rows=rows: e.matmul(
                        b[:, 0:256], SMALL[rows, kcol:kcol + 128], SMALL[rows, qcol:qcol + 256],
                        start=True, stop=True), inc=s_tc,
                        after=tok_bar + (win_ps_free.get(4 + kc) or []))
                texps = []
                for kc in kcs:
                    te = k.emit("s", lambda e, kc=kc, b=PS[4 + kc], ebuf=ebuf: e.activation(
                        ebuf[:, kc * 256:(kc + 1) * 256], b[:, 0:256], AF.Exp, scale=8.0),
                        inc=s_sk, after=[sc_mm[kc]] + ebuf_free)
                    texps.append(te)
                    win_ps_free[4 + kc] = [te]
                tmask = []
                for kc in kcs:
                    for j in range(2):
                        if lam_part[(w, j, kc)]:
                            lcol = C_LAM + (j * 4 + kc) * 128
                            tmask.append(k.emit("v", lambda e, kc=kc, lcol=lcol, j=j, ebuf=ebuf: e.tensor_tensor(
                                ebuf[:, kc * 256 + j * 128:kc * 256 + (j + 1) * 128],
                                ebuf[:, kc * 256 + j * 128:kc * 256 + (j + 1) * 128],
                                CONST[:, lcol:lcol + 128], ALU.mult),
                                inc=s_vc, after=texps))
                tok_e = texps + tmask
                for j in range(2):
                    vk = [kc for kc in kcs if lam_full[(w, j, kc)] or lam_part[(w, j, kc)]]
                    bankA, bankS = PS[hp], PS[2 + hp]
                    mmS = None
                    for ii, kc in enumerate(vk):
                        wslot = ((w - 1 + kc // 2) % 2)
                        vcol = vbf + wslot * 2048 + (kc % 2) * 1024 + h * 64
                        aftA = tok_e + ((win_ps_free.get(hp) or []) if ii == 0 else [])
                        k.emit("t", lambda e, b=bankA, kc=kc, j=j, vcol=vcol, ebuf=ebuf,
                               st=(ii == 0), sp=(ii == len(vk) - 1): e.matmul(
                            b[:, 0:64], ebuf[:, kc * 256 + j * 128:kc * 256 + (j + 1) * 128],
                            BIG[:, vcol:vcol + 64], start=st, stop=sp),
                            inc=None, after=aftA)
                        aftS = ((win_ps_free.get(2 + hp) or []) if ii == 0 else [])
                        mmS = k.emit("t", lambda e, b=bankS, kc=kc, j=j, ebuf=ebuf,
                                     st=(ii == 0), sp=(ii == len(vk) - 1): e.matmul(
                            b[:, 0:1], ebuf[:, kc * 256 + j * 128:kc * 256 + (j + 1) * 128],
                            CONST[:, ctx["C_ONES"]:ctx["C_ONES"] + 1], start=st, stop=sp),
                            inc=s_tc if ii == len(vk) - 1 else None, after=aftS)
                    trc = k.emit("v", lambda e, b=bankS, hp=hp: e.reciprocal(
                        STAT[:, 24 + hp:25 + hp], b[:, 0:1]), inc=s_vc, after=[mmS])
                    acol = awin + j * 1024 + h * 64
                    tev = k.emit("v", lambda e, b=bankA, acol=acol, hp=hp: e.tensor_scalar(
                        BIG[:, acol:acol + 64], b[:, 0:64], STAT[:, 24 + hp:25 + hp],
                        None, ALU.mult), inc=s_vc, after=[trc])
                    win_ps_free[hp] = [tev]
                    win_ps_free[2 + hp] = [tev]
                ebuf_free = [s_tc.tok(), s_vc.tok()]

            tok_bar = barrier()
            for jt in range(2):
                tt = tiles[jt]
                opT = XA[:, 5128:6152]
                tevs2 = []
                trps_free = [None, None]
                for dc in range(NDC):
                    bp = dc % 2
                    bank = PS[2 + bp]
                    aft = tok_bar + ([trps_free[bp]] if trps_free[bp] else [])
                    if jt == 1 and dc < 2:
                        aft = aft + [s_tc.tok(), s_vc.tok(), s_sk.tok()]
                    ttr = k.emit("t", lambda e, jt=jt, dc=dc, b=bank: e.transpose(
                        b[:, 0:128], BIG[:, awin + jt * 1024 + dc * 128:awin + jt * 1024 + (dc + 1) * 128],
                        ident_ap), inc=s_tc, after=aft)
                    eng, ch = ("v", s_vc) if dc % 2 == 0 else ("s", s_sk)
                    fcp = (lambda e, b=bank, dc=dc, opT=opT: e.tensor_copy(
                        opT[:, dc * 128:(dc + 1) * 128], b[:, 0:128])) if eng == "v" else \
                          (lambda e, b=bank, dc=dc, opT=opT: e.copy(opT[:, dc * 128:(dc + 1) * 128], b[:, 0:128]))
                    tev = k.emit(eng, fcp, inc=ch, after=[ttr])
                    trps_free[bp] = tev
                    tevs2.append(tev)
                for eo in range(2):
                    mm = None
                    for dc in range(NDC):
                        par = dc % 2
                        k.dma("y", MID[:, 3500 + par * 512:3500 + (par + 1) * 512],
                              bass.AP(H["f_wo"], li * DM * DM + dc * 128 * DM + eo * 512,
                                      [[DM, 128], [1, 512]]),
                              s_wo[par], after=([wo_free[par]] if wo_free[par] else []) + [tok_cl] + tok_bar)
                        tok_wo = s_wo[par].tok()
                        bank = PS[6 + eo]
                        aft = [tok_wo] + tevs2 + ((win_ps_free.get(6 + eo) or []) if dc == 0 else [])
                        mm = k.emit("t", lambda e, b=bank, dc=dc, par=par, opT=opT: e.matmul(
                            b[:, :], opT[:, dc * 128:(dc + 1) * 128],
                            MID[:, 3500 + par * 512:3500 + (par + 1) * 512],
                            start=(dc == 0), stop=(dc == NDC - 1)),
                            inc=s_tc, after=aft)
                        wo_free[par] = s_tc.tok()
                    tadd = k.emit("v", lambda e, tt=tt, eo=eo, b=PS[6 + eo]: e.tensor_tensor(
                        RES[:, tt * DM + eo * 512:tt * DM + (eo + 1) * 512],
                        RES[:, tt * DM + eo * 512:tt * DM + (eo + 1) * 512],
                        b[:, :], ALU.add), inc=s_vc, after=[mm])
                    win_ps_free[6 + eo] = [tadd]

        if upto == "p2a":
            break
        tok_bar = barrier()
        for tt in range(NT):
            k.dma("g", SMALL[tt:tt + 1, 0:512], RES[127:128, tt * DM + 512:(tt + 1) * DM],
                  s_bd, after=tok_bar)
        tok_bd = s_bd.tok()

        w1_free = [None, None]
        w2w_free = [None, None]
        for cc in range(4):
            tok_bar = barrier([tok_bd])
            for lt in range(4):
                tt = cc * 4 + lt
                xs = BIG[:, 13312:14336]
                aft_xs = tok_bar + ([s_tc.tok()] if lt > 0 else [])
                tc1 = k.emit("v", lambda e, tt=tt, xs=xs: e.tensor_copy(
                    xs[:, 0:512], RES[:, tt * DM:tt * DM + 512]), inc=s_vc, after=aft_xs)
                if tt == 0:
                    tok_sh0 = k.emit("v", lambda e, xs=xs: e.memset(xs[0:1, 512:1024], 0.0),
                                     inc=s_vc, after=aft_xs)
                    k.waits("s", [tok_sh0])
                else:
                    tok_sh0 = k.dma("s", xs[0:1, 512:1024], SMALL[tt - 1:tt, 0:512],
                                    s_xs, after=aft_xs + [tok_bd, tc1])
                tok_sh1 = k.dma("s", xs[1:128, 512:1024], RES[0:127, tt * DM + 512:(tt + 1) * DM],
                                s_xs, after=aft_xs + [tc1])
                tok_xs = s_xs.tok()
                sm, mu, vv, rs = (STAT[:, 28:29], STAT[:, 29:30], STAT[:, 30:31], STAT[:, 31:32])
                disc = BIG[:, 14848:15872]
                aft_l1 = [tc1, tok_xs] + ([tok_sh0] if tt == 0 else [])
                l1 = k.emit("v", lambda e, xs=xs, sm=sm: e.reduce_sum(sm, xs, axis=AX),
                            inc=s_vc, after=aft_l1)
                l2 = k.emit("v", lambda e, sm=sm, mu=mu: e.tensor_scalar(
                    mu, sm, 1.0 / DM, None, ALU.mult), inc=s_vc, after=[l1])
                l3 = k.emit("v", lambda e, xs=xs, mu=mu: e.tensor_scalar(
                    xs, xs, mu, None, ALU.subtract), inc=s_vc, after=[l2])
                l4 = k.emit("v", lambda e, xs=xs, disc=disc, vv=vv: e.scalar_tensor_tensor(
                    disc, xs, 1.0, xs, ALU.mult, ALU.mult, accum_out=vv), inc=s_vc, after=[l3])
                l5 = k.emit("v", lambda e, vv=vv: e.tensor_scalar(
                    vv, vv, 1.0 / DM, CONST[:, ctx["C_KONST"]:ctx["C_KONST"] + 1],
                    ALU.mult, ALU.add), inc=s_vc, after=[l4])
                l6 = k.emit("s", lambda e, vv=vv, rs=rs: e.activation(rs, vv, AF.Sqrt),
                            inc=s_sk, after=[l5])
                l7 = k.emit("v", lambda e, rs=rs: e.reciprocal(rs, rs), inc=s_vc, after=[l6])
                l8 = k.emit("v", lambda e, xs=xs, rs=rs: e.tensor_scalar(
                    xs, xs, rs, None, ALU.mult), inc=s_vc, after=[l7])
                l9 = k.emit("v", lambda e, xs=xs: e.tensor_tensor(
                    xs, xs, MID[:, 384:1408], ALU.mult), inc=s_vc, after=[l8])
                l10 = k.emit("v", lambda e, xs=xs: e.tensor_tensor(
                    xs, xs, MID[:, 1408:2432], ALU.add), inc=s_vc, after=[l9])
                trps_free = [None, None]
                for dc in range(NDC):
                    bp = dc % 2
                    bank = PS[2 + bp]
                    ttr = k.emit("t", lambda e, xs=xs, dc=dc, b=bank: e.transpose(
                        b[:, 0:128], xs[:, dc * 128:(dc + 1) * 128], ident_ap),
                        inc=s_tc, after=[l10] + ([trps_free[bp]] if trps_free[bp] else []))
                    dst = dc * 512 + lt * 128
                    eng, ch = ("v", s_vc) if dc % 2 == 0 else ("s", s_sk)
                    fcp = (lambda e, b=bank, dst=dst: e.tensor_copy(
                        XA[:, dst:dst + 128], b[:, 0:128])) if eng == "v" else \
                          (lambda e, b=bank, dst=dst: e.copy(XA[:, dst:dst + 128], b[:, 0:128]))
                    tev = k.emit(eng, fcp, inc=ch, after=[ttr])
                    trps_free[bp] = tev
            tok_bar = barrier()
            hps_free = [None, None]
            gelu_prev = None
            for hc in range(44):
                par = hc % 2
                hsrc = "f_w1a" if hc < 22 else "f_w1g"
                hh = hc if hc < 22 else hc - 22
                k.dma("y", BIG[:, par * 1024:(par + 1) * 1024].rearrange("p (c j) -> p c j", j=128),
                      bass.AP(H[hsrc], li * DM * 2816 + hh * 128, [[2816, 128], [128 * 2816, 8], [1, 128]]),
                      s_w1[par], after=([w1_free[par]] if w1_free[par] else []) + [tok_cl] + tok_bar)
                tok_w1 = s_w1[par].tok()
                bank = PS[par]
                mm = None
                for dc in range(NDC):
                    aft = [tok_w1] + (([hps_free[par]] if hps_free[par] else []) if dc == 0 else [])
                    mm = k.emit("t", lambda e, b=bank, dc=dc, par=par: e.matmul(
                        b[:, :], BIG[:, par * 1024 + dc * 128:par * 1024 + (dc + 1) * 128],
                        XA[:, dc * 512:(dc + 1) * 512],
                        start=(dc == 0), stop=(dc == NDC - 1)),
                        inc=s_tc if dc == NDC - 1 else None, after=aft)
                w1_free[par] = s_tc.tok()
                if hc < 22:
                    tev = k.emit("v", lambda e, b=bank, hc=hc: e.tensor_scalar(
                        BIG[:, 2048 + hc * 512:2048 + (hc + 1) * 512], b[:, :],
                        MID[:, 2432 + hc:2433 + hc], None, ALU.add), inc=s_vc, after=[mm])
                    hps_free[par] = tev
                else:
                    g1 = k.emit("s", lambda e, b=bank, hh=hh: e.activation(
                        BIG[:, 14336:14848], b[:, :], AF.Gelu,
                        bias=MID[:, 2454 + hh:2455 + hh], scale=1.0),
                        inc=s_sk, after=[mm] + ([gelu_prev] if gelu_prev else []))
                    g2 = k.emit("v", lambda e, hh=hh: e.tensor_tensor(
                        BIG[:, 2048 + hh * 512:2048 + (hh + 1) * 512],
                        BIG[:, 2048 + hh * 512:2048 + (hh + 1) * 512],
                        BIG[:, 14336:14848], ALU.mult), inc=s_vc, after=[g1])
                    gelu_prev = g2
                    hps_free[par] = g2
            tok_bar = barrier()
            for hc in range(22):
                par = hc % 2
                k.dma("y", XA[:, 4104 + par * 1024:4104 + (par + 1) * 1024],
                      bass.AP(H["f_w2"], li * 2816 * DM + hc * 128 * DM, [[DM, 128], [1, 1024]]),
                      s_w2w[par], after=([w2w_free[par]] if w2w_free[par] else []) + [tok_cl] + tok_bar)
                tok_w2 = s_w2w[par].tok()
                for lt in range(4):
                    for eo in range(2):
                        bank = PS[lt * 2 + eo]
                        aft = [tok_w2] + (tok_bar if hc == 0 else [])
                        k.emit("t", lambda e, b=bank, hc=hc, lt=lt, eo=eo, par=par: e.matmul(
                            b[:, :], BIG[:, 2048 + hc * 512 + lt * 128:2048 + hc * 512 + (lt + 1) * 128],
                            XA[:, 4104 + par * 1024 + eo * 512:4104 + par * 1024 + (eo + 1) * 512],
                            start=(hc == 0), stop=(hc == 21)),
                            inc=s_tc if (lt == 3 and eo == 1) else None,
                            after=aft)
                w2w_free[par] = s_tc.tok()
            tok_w2done = s_tc.tok()
            for lt in range(4):
                tt = cc * 4 + lt
                for eo in range(2):
                    bank = PS[lt * 2 + eo]
                    tmp = BIG[:, 13312 + eo * 512:13312 + (eo + 1) * 512]
                    o1 = k.emit("v", lambda e, b=bank, tmp=tmp, eo=eo: e.scalar_tensor_tensor(
                        tmp, MID[:, 2476 + eo * 512:2476 + (eo + 1) * 512], 1.0, b[:, :],
                        ALU.mult, ALU.add), inc=s_vc, after=[tok_w2done])
                    o2 = k.emit("v", lambda e, tt=tt, eo=eo, tmp=tmp: e.tensor_tensor(
                        RES[:, tt * DM + eo * 512:tt * DM + (eo + 1) * 512],
                        RES[:, tt * DM + eo * 512:tt * DM + (eo + 1) * 512],
                        tmp, ALU.add), inc=s_vc, after=[o1])

    tok_bar = barrier()
    for tt in range(NT):
        k.dma("g", H["s_ffn0"].ap()[tt * 128:(tt + 1) * 128, :], RES[:, tt * DM:(tt + 1) * DM],
              s_fs, after=tok_bar)
    ctx["tok_ffn0"] = s_fs.tok()
    ctx["tok_big_free"] = s_tc.tok()
    ctx["tok_res_free"] = [s_fs.tok()]


def emit_p4(ctx, upto):
    k = ctx["k"]
    H, PS = ctx["H"], ctx["PS"]
    XA, RES, BIG, MID, SMALL, CONST, STAT = (ctx["XA"], ctx["RES"], ctx["BIG"], ctx["MID"],
                                             ctx["SMALL"], ctx["CONST"], ctx["STAT"])
    s_tc, s_vc, s_sk = ctx["s_tc"], ctx["s_vc"], ctx["s_sk"]
    ident_ap = CONST[:, ctx["C_IDENT"]:ctx["C_IDENT"] + 128]
    ones_col = CONST[0:64, ctx["C_ONES"]:ctx["C_ONES"] + 1]

    def barrier(dma_toks=()):
        toks = [s_tc.tok(), s_vc.tok(), s_sk.tok()] + [t for t in dma_toks if t]
        for e in ("t", "v", "s", "g", "y"):
            k.waits(e, toks)
        return toks

    ent = barrier([ctx["tok_x1_hbm"]])

    s_4l = k.sem("p4_ld")
    s_qg = k.sem("p4_qg")
    s_wcl = k.sem("p4_wcl")
    s_bq = k.sem("p4_bq")
    s_bo = k.sem("p4_bo")
    s_arr = k.sem("p4_arr")
    s_mcol = k.sem("p4_mcol")
    s_bp = k.sem("p4_bp")
    s_xn1s = [k.sem("p4_xn1s0"), k.sem("p4_xn1s1")]
    s_gw = k.sem("p4_gw")
    s_spl = [k.sem("p4_spl0"), k.sem("p4_spl1")]
    s_srow = k.sem("p4_srow")
    s_mrow = k.sem("p4_mrow")
    s_xsl = k.sem("p4_xsl")
    s_aw = [k.sem("p4_aw0"), k.sem("p4_aw1")]
    s_ql = [k.sem("p4_ql0"), k.sem("p4_ql1")]
    s_pw = [k.sem("p4_pw0"), k.sem("p4_pw1")]
    s_out = k.sem("p4_out")
    s_cc = ctx["s_cc"]

    slot_free = [None, None]
    bank_free = [None, None]
    for tt in range(NT):
        par = tt % 2
        xslot = XA[:, par * DM:(par + 1) * DM]
        aft = ent + (slot_free[par] or [])
        tln = ln_tile(ctx, RES[:, tt * DM:(tt + 1) * DM], xslot, 4 * par, 2048 + par * 1024,
                      after=aft)
        tok_store = k.dma("g", H["s_xn1"].ap()[tt * 128:(tt + 1) * 128, :], xslot,
                          s_xn1s[par], after=[tln])
        for dc in range(NDC):
            bp = dc % 2
            bank = PS[2 + bp]
            ttr = k.emit("t", lambda e, s=xslot, dc=dc, b=bank: e.transpose(
                b[:, 0:128], s[:, dc * 128:(dc + 1) * 128], ident_ap),
                inc=s_tc, after=[tln] + ([bank_free[bp]] if bank_free[bp] else []))
            dcol = dc * T + tt * 128
            eng, ch = ("v", s_vc) if dc % 2 == 0 else ("s", s_sk)
            if eng == "v":
                tev = k.emit(eng, lambda e, b=bank, dcol=dcol: e.tensor_copy(
                    BIG[:, dcol:dcol + 128], b[:, 0:128]), inc=ch, after=[ttr])
            else:
                tev = k.emit(eng, lambda e, b=bank, dcol=dcol: e.copy(
                    BIG[:, dcol:dcol + 128], b[:, 0:128]), inc=ch, after=[ttr])
            bank_free[bp] = tev
        slot_free[par] = [tok_store, s_tc.tok()]
    tok_xn1 = barrier([s_xn1s[0].tok(), s_xn1s[1].tok()])

    k.dma("y", STAT[0:16, 48:112], H["g1_qg"].ap(), s_qg, after=tok_xn1)
    tok_qg = s_qg.tok()
    g1 = k.emit("v", lambda e: e.tensor_tensor(XA[0:16, 0:64], STAT[0:16, 48:112],
                                               STAT[0:16, 48:112], ALU.mult),
                inc=s_vc, after=[tok_qg] + tok_xn1)
    g2 = k.emit("v", lambda e: e.reduce_sum(STAT[0:16, 112:113], XA[0:16, 0:64], axis=AX),
                inc=s_vc, after=[g1])
    g3 = k.emit("s", lambda e: e.activation(STAT[0:16, 113:114], STAT[0:16, 112:113], AF.Sqrt),
                inc=s_sk, after=[g2])
    g4 = k.emit("v", lambda e: e.reciprocal(STAT[0:16, 113:114], STAT[0:16, 113:114]),
                inc=s_vc, after=[g3])
    g5 = k.emit("v", lambda e: e.tensor_scalar(XA[0:16, 0:64], STAT[0:16, 48:112],
                                               STAT[0:16, 113:114], None, ALU.mult),
                inc=s_vc, after=[g4])
    g6 = k.emit("t", lambda e: e.transpose(PS[6][0:64, 0:16], XA[0:16, 0:64],
                                           ident_ap[0:16, 0:16]),
                inc=s_tc, after=[g5] + tok_xn1)
    g7 = k.emit("v", lambda e: e.tensor_copy(STAT[0:64, 32:48], PS[6][0:64, 0:16]),
                inc=s_vc, after=[g6])

    zs = k.emit("v", lambda e: e.memset(XA[0:1, 4096:6144], 0.0), inc=s_vc, after=tok_xn1 + [g5])
    gw_free = None
    prev_evs = []
    sim_prev = zs
    spl_free = [None, None]
    for h in range(NH):
        for part in range(3):
            k.dma("y", bass.AP(MID, part * 64, [[4608, 128], [192, 8], [1, 64]]),
                  bass.AP(H["g1_wqkv"], part * DM + h * 64,
                          [[3 * DM, 128], [128 * 3 * DM, 8], [1, 64]]),
                  s_gw, after=((gw_free or []) + tok_xn1) if part == 0 else ())
        tok_gw = s_gw.tok()
        evs = []
        for which in range(2):
            for tc4 in range(4):
                bank = PS[which]
                mm = None
                for dc in range(NDC):
                    mm = k.emit("t", lambda e, b=bank, w=which, dc=dc, tc4=tc4: e.matmul(
                        b[0:64, :], MID[:, dc * 192 + w * 64:dc * 192 + (w + 1) * 64],
                        BIG[:, dc * T + tc4 * 512:dc * T + (tc4 + 1) * 512],
                        start=(dc == 0), stop=(dc == NDC - 1)),
                        inc=s_tc if dc == NDC - 1 else None,
                        after=([tok_gw] + prev_evs + evs[-2:]) if dc == 0 else ())
                dst = XA[0:64, which * 2048 + tc4 * 512:which * 2048 + (tc4 + 1) * 512]
                tev = k.emit("v", lambda e, b=bank, d=dst: e.tensor_copy(d, b[0:64, :]),
                             inc=s_vc, after=[mm])
                evs.append(tev)
        for tt in range(NT):
            bank = PS[2 + tt % 2]
            mm = None
            for dc in range(NDC):
                mm = k.emit("t", lambda e, b=bank, dc=dc, tt=tt: e.matmul(
                    b[:, 0:64], BIG[:, dc * T + tt * 128:dc * T + (tt + 1) * 128],
                    MID[:, dc * 192 + 128:dc * 192 + 192],
                    start=(dc == 0), stop=(dc == NDC - 1)),
                    inc=s_tc if dc == NDC - 1 else None,
                    after=([tok_gw] + prev_evs + evs[-2:]) if dc == 0 else ())
            tev = k.emit("s", lambda e, b=bank, tt=tt: e.copy(
                SMALL[:, tt * 64:(tt + 1) * 64], b[:, 0:64]),
                inc=s_sk, after=[mm] + prev_evs)
            evs.append(tev)
        gw_free = [s_tc.tok()]
        sp = h % 2
        tq = k.dma("g", H["s_l1q"].ap()[h], XA[0:64, 0:2048], s_spl[sp],
                   after=evs[0:4] + (spl_free[sp] or []))
        tk = k.dma("g", H["s_l1k"].ap()[h], XA[0:64, 2048:4096], s_spl[sp], after=evs[4:8])
        tv = k.dma("g", bass.AP(H["s_l1v"], h * T * DH, [[DH, 128], [128 * DH, 16], [1, 64]]),
                   SMALL[:, 0:1024].rearrange("p (t d) -> p t d", d=64), s_spl[sp],
                   after=evs[8:])
        spl_free[sp] = [s_spl[sp].tok()]
        for tc4 in range(4):
            bs = PS[4 + tc4 % 2]
            bq = PS[6 + tc4 % 2]
            m1 = k.emit("t", lambda e, b=bs, tc4=tc4, h=h: e.matmul(
                b[0:1, 0:512], STAT[0:64, 32 + h:33 + h],
                XA[0:64, tc4 * 512:(tc4 + 1) * 512], start=True, stop=True),
                inc=None, after=[g7] + evs[0:4] + ([sim_prev] if tc4 < 2 else []))
            sq = k.emit("v", lambda e, tc4=tc4: e.tensor_tensor(
                SMALL[0:64, 1024 + tc4 * 512:1024 + (tc4 + 1) * 512],
                XA[0:64, tc4 * 512:(tc4 + 1) * 512],
                XA[0:64, tc4 * 512:(tc4 + 1) * 512], ALU.mult), inc=s_vc, after=evs[0:4])
            m2 = k.emit("t", lambda e, b=bq, tc4=tc4: e.matmul(
                b[0:1, 0:512], ones_col,
                SMALL[0:64, 1024 + tc4 * 512:1024 + (tc4 + 1) * 512],
                start=True, stop=True), inc=s_tc, after=[sq])
            e1 = k.emit("s", lambda e, b=bq: e.activation(
                XA[0:1, 6144:6656], b[0:1, 0:512], AF.Sqrt), inc=s_sk,
                after=[m2, sim_prev])
            e2 = k.emit("v", lambda e: e.reciprocal(XA[0:1, 6144:6656], XA[0:1, 6144:6656]),
                        inc=s_vc, after=[e1])
            e3 = k.emit("v", lambda e, b=bs: e.tensor_tensor(
                XA[0:1, 6144:6656], XA[0:1, 6144:6656], b[0:1, 0:512], ALU.mult),
                inc=s_vc, after=[e2, m1])
            e4 = k.emit("v", lambda e, tc4=tc4: e.tensor_tensor(
                XA[0:1, 4096 + tc4 * 512:4096 + (tc4 + 1) * 512],
                XA[0:1, 4096 + tc4 * 512:4096 + (tc4 + 1) * 512],
                XA[0:1, 6144:6656], ALU.add), inc=s_vc, after=[e3])
            sim_prev = e4
        prev_evs = [s_vc.tok(), s_sk.tok(), s_tc.tok(), sim_prev] + (spl_free[sp] or [])
    mn = STAT[0:1, 114:115]
    mx = STAT[0:1, 115:116]
    t1 = k.emit("v", lambda e: e.tensor_reduce(mn, XA[0:1, 4101:6144], AX, ALU.min),
                inc=s_vc, after=[sim_prev])
    t2 = k.emit("v", lambda e: e.reduce_max(mx, XA[0:1, 4101:6144], axis=AX),
                inc=s_vc, after=[sim_prev])
    t3 = k.emit("v", lambda e: e.tensor_scalar(STAT[0:1, 116:117], mn, -1.0, None, ALU.mult),
                inc=s_vc, after=[t1])
    t4 = k.emit("v", lambda e: e.tensor_copy(STAT[0:1, 117:118], mx), inc=s_vc, after=[t2])
    tar = k.dma("g", H["s_ar_in"].ap(), STAT[0:1, 116:118], s_srow, after=[t3, t4])
    k.waits("g", [tar])
    tok_ar = k.emit("g", lambda e: e.collective_compute(
        "AllReduce", ALU.max, replica_groups=[list(range(NCORE))],
        ins=[H["s_ar_in"].ap().opt()], outs=[H["s_ar_out"].ap().opt()]), inc=s_cc)
    tok_simrow = k.dma("g", bass.AP(H["s_row"], 0, [[1, 2048]]),
                       XA[0:1, 4096:6144], s_srow, after=[sim_prev, t1, t2])
    tok_srow = s_srow.tok()

    if upto == "p4b":
        return
    tok_c0 = barrier([tok_srow, s_spl[0].tok(), s_spl[1].tok(), s_gw.tok()])
    for i, t_idx in enumerate(IDX200):
        dst = MID[i:i + 1, 0:1024] if i < 128 else MID[i - 128:i - 127, 1024:2048]
        k.dma("y", dst, H["s_xn1"].ap()[int(t_idx):int(t_idx) + 1, :], s_xsl, after=tok_c0)
    tok_xsl = s_xsl.tok()
    k.dma("y", MID[0:20, 2688:2752], H["a1_wc"].ap()[0:20, :], s_wcl, after=tok_c0)
    k.dma("y", MID[0:20, 2752:2816], H["a1_wc"].ap()[20:40, :], s_wcl)
    k.dma("y", MID[:, 2816:2880], bass.AP(H["a1_bc"], 0, [[0, 128], [1, 64]]), s_wcl)
    tok_wc = s_wcl.tok()
    wewr_ev = []
    aw_free = [None, None]
    for h in range(NH):
        for wi, wname in enumerate(("a1_we", "a1_wr")):
            bank = PS[wi]
            k.dma("y", SMALL[0:128, 4096 + wi * 1024:4096 + wi * 1024 + 20],
                  H[wname].ap()[h, 0:128, :], s_aw[wi], after=(aw_free[wi] or []) + tok_c0)
            k.dma("y", SMALL[0:72, 4096 + wi * 1024 + 20:4096 + wi * 1024 + 40],
                  H[wname].ap()[h, 128:200, :], s_aw[wi])
            tok_w = s_aw[wi].tok()
            m1 = k.emit("t", lambda e, b=bank, h=h, wi=wi: e.matmul(
                b[0:64, 0:20], MID[:, h * 64:(h + 1) * 64],
                SMALL[0:128, 4096 + wi * 1024:4096 + wi * 1024 + 20],
                start=True, stop=False), inc=None,
                after=[tok_w, tok_xsl] + (wewr_ev[-2:] if len(wewr_ev) >= 2 else []))
            m2 = k.emit("t", lambda e, b=bank, h=h, wi=wi: e.matmul(
                b[0:64, 0:20], MID[0:72, 1024 + h * 64:1024 + (h + 1) * 64],
                SMALL[0:72, 4096 + wi * 1024 + 20:4096 + wi * 1024 + 40],
                start=False, stop=True), inc=s_tc, after=())
            tev = k.emit("v", lambda e, b=bank, h=h, wi=wi: e.tensor_copy(
                MID[0:64, 2048 + h * 40 + wi * 20:2048 + h * 40 + (wi + 1) * 20],
                b[0:64, 0:20]), inc=s_vc, after=[m2])
            wewr_ev.append(tev)
            aw_free[wi] = [s_tc.tok()]
    tok_wewr = barrier([s_aw[0].tok(), s_aw[1].tok()])
    aw_free = [None, None]
    pw_free = [None, None]
    out_prev = []
    for g in range(8):
        for wi, wname in enumerate(("a1_wq", "a1_wk")):
            k.dma("y", SMALL[:, 4096 + wi * 1024:5120 + wi * 1024].rearrange(
                "p (c j) -> p c j", j=128),
                bass.AP(H[wname], g * 128, [[DM, 128], [128 * DM, 8], [1, 128]]),
                s_aw[wi], after=(aw_free[wi] or []) + tok_wewr)
        tok_w = [s_aw[0].tok(), s_aw[1].tok()]
        k.dma("y", MID[:, 2880:3008], bass.AP(H["a1_bq_row"], g * 128, [[0, 128], [1, 128]]),
              s_bq, after=tok_wewr + out_prev)
        k.dma("y", MID[:, 3008:3136], bass.AP(H["a1_bk_row"], g * 128, [[0, 128], [1, 128]]),
              s_bq)
        tok_b = s_bq.tok()
        qk_ev = []
        for wi in range(2):
            for tt in range(NT):
                bank = PS[6 + tt % 2]
                mm = None
                for dc in range(NDC):
                    mm = k.emit("t", lambda e, b=bank, wi=wi, dc=dc, tt=tt: e.matmul(
                        b[:, 0:128], BIG[:, dc * T + tt * 128:dc * T + (tt + 1) * 128],
                        SMALL[:, 4096 + wi * 1024 + dc * 128:4096 + wi * 1024 + (dc + 1) * 128],
                        start=(dc == 0), stop=(dc == NDC - 1)),
                        inc=s_tc if dc == NDC - 1 else None,
                        after=(tok_w + out_prev + (qk_ev[-2:] if len(qk_ev) >= 2 else [])) if dc == 0 else ())
                tev = k.emit("v", lambda e, b=bank, wi=wi, tt=tt: e.tensor_tensor(
                    XA[:, wi * 2048 + tt * 128:wi * 2048 + (tt + 1) * 128],
                    b[:, 0:128], MID[:, 2880 + wi * 128:2880 + (wi + 1) * 128], ALU.add),
                    inc=s_vc, after=[mm, tok_b])
                qk_ev.append(tev)
        tok_qk = barrier()
        for hh in range(2):
            h = g * 2 + hh
            for wi in range(2):
                for tt in range(NT):
                    hcol = wi * 2048 + tt * 128 + hh * 64
                    seg = XA[:, hcol:hcol + 64]
                    u1 = k.emit("v", lambda e, seg=seg: e.scalar_tensor_tensor(
                        XA[:, 6144:6208], seg, 1.0, seg, ALU.mult, ALU.mult,
                        accum_out=STAT[:, 12:13]), inc=s_vc, after=tok_qk)
                    u2 = k.emit("s", lambda e: e.activation(STAT[:, 13:14], STAT[:, 12:13],
                                                            AF.Sqrt), inc=s_sk, after=[u1])
                    u3 = k.emit("v", lambda e: e.tensor_scalar(STAT[:, 13:14], STAT[:, 13:14],
                                                               1e-12, None, ALU.max),
                                inc=s_vc, after=[u2])
                    u4 = k.emit("v", lambda e: e.reciprocal(STAT[:, 13:14], STAT[:, 13:14]),
                                inc=s_vc, after=[u3])
                    u5 = k.emit("v", lambda e, seg=seg: e.tensor_scalar(
                        seg, seg, STAT[:, 13:14], None, ALU.mult), inc=s_vc, after=[u4])
                    bank = PS[2 + tt % 2]
                    ttr = k.emit("t", lambda e, seg=seg, b=bank: e.transpose(
                        b[0:64, 0:128], seg, ident_ap), inc=s_tc, after=[u5])
                    dst = SMALL[0:64, wi * 2048 + tt * 128:wi * 2048 + (tt + 1) * 128]
                    tev = k.emit("s", lambda e, b=bank, d=dst: e.copy(d, b[0:64, 0:128]),
                                 inc=s_sk, after=[ttr])
            tok_n = barrier()
            oh_ev = []
            for tc4 in range(4):
                me = k.emit("t", lambda e, tc4=tc4, h=h: e.matmul(
                    PS[0][0:20, 0:512],
                    MID[0:64, 2048 + h * 40:2048 + h * 40 + 20],
                    SMALL[0:64, tc4 * 512:(tc4 + 1) * 512],
                    start=True, stop=True), inc=s_tc,
                    after=tok_n + (oh_ev[-2:] if len(oh_ev) >= 2 else []))
                mr = k.emit("t", lambda e, tc4=tc4, h=h: e.matmul(
                    PS[1][0:20, 0:512],
                    MID[0:64, 2048 + h * 40 + 20:2048 + h * 40 + 40],
                    SMALL[0:64, 2048 + tc4 * 512:2048 + (tc4 + 1) * 512],
                    start=True, stop=True), inc=s_tc, after=())
                ee = k.emit("v", lambda e: e.tensor_copy(XA[0:20, 5120:5632], PS[0][0:20, 0:512]),
                            inc=s_vc, after=[me, mr] + oh_ev[-8:])
                er = k.emit("s", lambda e: e.copy(XA[0:20, 5632:6144], PS[1][0:20, 0:512]),
                            inc=s_sk, after=[mr] + oh_ev[-8:])
                for lt in range(4):
                    tt = tc4 * 4 + lt
                    bank = PS[6 + lt % 2]
                    m1 = k.emit("t", lambda e, b=bank, lt=lt: e.matmul(
                        b[:, 0:64], XA[0:20, 5120 + lt * 128:5120 + (lt + 1) * 128],
                        MID[0:20, 2688:2752], start=True, stop=False), inc=None,
                        after=[ee, er, tok_wc] + (oh_ev[-2:] if len(oh_ev) >= 2 else []))
                    m2 = k.emit("t", lambda e, b=bank, lt=lt: e.matmul(
                        b[:, 0:64], XA[0:20, 5632 + lt * 128:5632 + (lt + 1) * 128],
                        MID[0:20, 2752:2816], start=False, stop=True), inc=s_tc, after=())
                    tev = k.emit("v", lambda e, b=bank, tt=tt: e.tensor_tensor(
                        XA[:, 4096 + tt * 64:4096 + (tt + 1) * 64], b[:, 0:64],
                        MID[:, 2816:2880], ALU.add), inc=s_vc, after=[m2])
                    oh_ev.append(tev)
            tok_oh = barrier()
            ot_ev = []
            for tt in range(NT):
                bank = PS[2 + tt % 2]
                ttr = k.emit("t", lambda e, tt=tt, b=bank: e.transpose(
                    b[0:64, 0:128], XA[:, 4096 + tt * 64:4096 + (tt + 1) * 64], ident_ap),
                    inc=s_tc, after=tok_oh + (ot_ev[-2:] if len(ot_ev) >= 2 else []))
                tev = k.emit("s", lambda e, b=bank, tt=tt: e.copy(
                    SMALL[0:64, 4096 + tt * 128:4096 + (tt + 1) * 128], b[0:64, 0:128]),
                    inc=s_sk, after=[ttr])
                ot_ev.append(tev)
            tok_ot = barrier()
            for eo in range(2):
                par = eo
                k.dma("y", MID[0:64, 3136 + par * 512:3136 + (par + 1) * 512],
                      bass.AP(H["a1_wo"], h * 64 * DM + eo * 512, [[DM, 64], [1, 512]]),
                      s_pw[par], after=(pw_free[par] or []) + tok_ot)
                tok_pw = s_pw[par].tok()
                padd_prev = ctx.get("_p4c_wo_prev")
                for tt in range(NT):
                    bank = PS[4 + tt % 2]
                    mm = k.emit("t", lambda e, b=bank, tt=tt, par=par: e.matmul(
                        b[:, :], SMALL[0:64, 4096 + tt * 128:4096 + (tt + 1) * 128],
                        MID[0:64, 3136 + par * 512:3136 + (par + 1) * 512],
                        start=True, stop=True), inc=s_tc,
                        after=[tok_pw] + tok_ot + ([padd_prev] if padd_prev else []))
                    padd = k.emit("v", lambda e, b=bank, tt=tt, eo=eo: e.tensor_tensor(
                        RES[:, tt * DM + eo * 512:tt * DM + (eo + 1) * 512],
                        RES[:, tt * DM + eo * 512:tt * DM + (eo + 1) * 512],
                        b[:, :], ALU.add), inc=s_vc, after=[mm])
                    if tt >= 1:
                        padd_prev = padd
                ctx["_p4c_wo_prev"] = padd_prev
                pw_free[par] = [s_tc.tok(), s_vc.tok()]
            out_prev = [s_vc.tok()]
        aw_free = [[s_tc.tok()], [s_tc.tok()]]
    tok_c = barrier()
    k.dma("y", MID[:, 0:1024], bass.AP(H["a1_bo"], 0, [[0, 128], [1, 1024]]), s_bo, after=tok_c)
    tok_bo = s_bo.tok()
    bo_prev = None
    for tt in range(NT):
        bo_prev = k.emit("v", lambda e, tt=tt: e.tensor_tensor(
            RES[:, tt * DM:(tt + 1) * DM], RES[:, tt * DM:(tt + 1) * DM],
            MID[:, 0:1024], ALU.add), inc=s_vc, after=[tok_bo])
    tok_attn1 = barrier()

    if upto == "p4c":
        return
    k.dma("y", STAT[0:1, 118:120], H["s_ar_out"].ap(), s_arr, after=[tok_ar] + tok_attn1)
    k.dma("y", XA[0:1, 4096:6144], bass.AP(H["s_row"], 0, [[1, 2048]]), s_arr, after=tok_attn1)
    tok_arr = s_arr.tok()
    d1 = k.emit("v", lambda e: e.tensor_scalar(STAT[0:1, 120:121], STAT[0:1, 118:119],
                                               -1.0, None, ALU.mult), inc=s_vc, after=[tok_arr])
    d2 = k.emit("v", lambda e: e.tensor_tensor(STAT[0:1, 121:122], STAT[0:1, 119:120],
                                               STAT[0:1, 120:121], ALU.subtract),
                inc=s_vc, after=[d1])
    d3 = k.emit("v", lambda e: e.reciprocal(STAT[0:1, 121:122], STAT[0:1, 121:122]),
                inc=s_vc, after=[d2])
    d4 = k.emit("v", lambda e: e.tensor_scalar(XA[0:1, 4101:6144], XA[0:1, 4101:6144],
                                               STAT[0:1, 120:121], STAT[0:1, 121:122],
                                               ALU.subtract, ALU.mult),
                inc=s_vc, after=[d3, tok_arr])
    d5 = k.emit("v", lambda e: e.tensor_scalar(XA[0:1, 5:2048], XA[0:1, 4101:6144],
                                               0.9, None, ALU.is_gt), inc=s_vc, after=[d4])
    d6 = k.emit("v", lambda e: e.tensor_scalar(XA[0:1, 2053:4096], XA[0:1, 4101:6144],
                                               0.9, None, ALU.is_le), inc=s_vc, after=[d5])
    d7 = k.emit("v", lambda e: e.memset(XA[0:1, 0:5], 1.0), inc=s_vc, after=[d6])
    d8 = k.emit("v", lambda e: e.memset(XA[0:1, 2048:2053], 1.0), inc=s_vc, after=[d7])
    tpr = k.dma("g", bass.AP(H["s_row"], 2048, [[1, 2048]]), XA[0:1, 0:2048],
                s_mrow, after=[d5, d7, d8])
    tnr = k.dma("g", bass.AP(H["s_row"], 4096, [[1, 2048]]), XA[0:1, 2048:4096],
                s_mrow, after=[d6, d8])
    k.waits("y", [s_mrow.tok()])
    tpc = k.dma("y", STAT[:, 96:112], bass.AP(H["s_row"], 2048, [[1, 128], [128, 16]]),
                s_mcol, allow_slow_non_contiguous=True)
    tnc = k.dma("y", STAT[:, 64:80],
                bass.AP(H["s_row"], 4096, [[1, 128], [128, 16]]),
                s_mcol, allow_slow_non_contiguous=True)
    tok_masks = s_mcol.tok()

    tok_e0 = barrier([tok_masks, s_mrow.tok()])
    ql_free = None
    et_free = [None, None]
    w1p_free = None
    wstage_free = None
    pw_free = [None, None]
    for h in range(NH):
        sp = h % 2
        lq = k.dma("y", XA[0:64, 0:2048], H["s_l1q"].ap()[h], s_ql[sp],
                   after=(ql_free or []) + tok_e0 + [s_spl[0].tok(), s_spl[1].tok()])
        lk = k.dma("y", XA[0:64, 2048:4096], H["s_l1k"].ap()[h], s_ql[sp], after=())
        lv = k.dma("y", SMALL[:, 0:1024].rearrange("p (t d) -> p t d", d=64),
                   bass.AP(H["s_l1v"], h * T * DH, [[DH, 128], [128 * DH, 16], [1, 64]]),
                   s_ql[sp], after=())
        tok_ql = s_ql[sp].tok()
        vb_ev = []
        for tt in range(NT):
            pcol = STAT[:, 96 + tt:97 + tt]
            ncol = STAT[:, 64 + tt:65 + tt]
            b1 = k.emit("v", lambda e, tt=tt, pcol=pcol: e.tensor_scalar(
                SMALL[:, 1024 + tt * 65:1024 + tt * 65 + 64], SMALL[:, tt * 64:(tt + 1) * 64],
                pcol, None, ALU.mult), inc=s_vc, after=[tok_ql, tok_masks] + (vb_ev[-1:] or []))
            b2 = k.emit("v", lambda e, tt=tt, pcol=pcol: e.tensor_scalar(
                SMALL[:, 1024 + tt * 65 + 64:1024 + tt * 65 + 65],
                CONST[:, ctx["C_ONES"]:ctx["C_ONES"] + 1], pcol, None, ALU.mult),
                inc=s_vc, after=[b1])
            b3 = k.emit("v", lambda e, tt=tt, ncol=ncol: e.tensor_scalar(
                SMALL[:, 2112 + tt * 65:2112 + tt * 65 + 64], SMALL[:, tt * 64:(tt + 1) * 64],
                ncol, None, ALU.mult), inc=s_vc, after=[b2])
            b4 = k.emit("v", lambda e, tt=tt, ncol=ncol: e.tensor_scalar(
                SMALL[:, 2112 + tt * 65 + 64:2112 + tt * 65 + 65],
                CONST[:, ctx["C_ONES"]:ctx["C_ONES"] + 1], ncol, None, ALU.mult),
                inc=s_vc, after=[b3])
            vb_ev.append(b4)
        tok_vb = s_vc.tok()
        for qt in range(NT):
            q0 = qt * 128
            lmm = []
            gate_l = [tok_ql] + ([t for t in et_free if t] if qt >= 1 else []) \
                + (pw_free[0] or []) + (pw_free[1] or [])
            for kc in range(4):
                lmm.append(k.emit("t", lambda e, b=PS[kc], kc=kc, q0=q0: e.matmul(
                    b[:, :], XA[0:64, q0:q0 + 128],
                    XA[0:64, 2048 + kc * 512:2048 + (kc + 1) * 512],
                    start=True, stop=True), inc=s_tc, after=gate_l))
            r1 = k.emit("v", lambda e: e.reduce_max(STAT[:, 12:13], PS[0][:, :], axis=AX),
                        inc=s_vc, after=[lmm[0]])
            r2 = k.emit("v", lambda e: e.reduce_max(STAT[:, 13:14], PS[1][:, :], axis=AX),
                        inc=s_vc, after=[lmm[1], r1])
            r3 = k.emit("v", lambda e: e.reduce_max(STAT[:, 14:15], PS[2][:, :], axis=AX),
                        inc=s_vc, after=[lmm[2], r2])
            r4 = k.emit("v", lambda e: e.reduce_max(STAT[:, 15:16], PS[3][:, :], axis=AX),
                        inc=s_vc, after=[lmm[3], r3])
            r5 = k.emit("v", lambda e: e.tensor_tensor(STAT[:, 12:13], STAT[:, 12:13],
                                                       STAT[:, 13:14], ALU.max),
                        inc=s_vc, after=[r4])
            r6 = k.emit("v", lambda e: e.tensor_tensor(STAT[:, 14:15], STAT[:, 14:15],
                                                       STAT[:, 15:16], ALU.max),
                        inc=s_vc, after=[r5])
            r7 = k.emit("v", lambda e: e.tensor_tensor(STAT[:, 12:13], STAT[:, 12:13],
                                                       STAT[:, 14:15], ALU.max),
                        inc=s_vc, after=[r6])
            r8 = k.emit("v", lambda e: e.tensor_scalar(STAT[:, 13:14], STAT[:, 12:13],
                                                       -0.125, None, ALU.mult),
                        inc=s_vc, after=[r7])
            texps = []
            for kc in range(4):
                te = k.emit("s", lambda e, kc=kc, b=PS[kc]: e.activation(
                    XA[:, 4096 + kc * 512:4096 + (kc + 1) * 512], b[:, :], AF.Exp,
                    bias=STAT[:, 13:14], scale=0.125), inc=s_sk, after=[r8, lmm[kc]])
                texps.append(te)
            mmA = mmB = None
            for kc16 in range(NT):
                ep = kc16 % 2
                bank = PS[4 + ep]
                ttr = k.emit("t", lambda e, kc16=kc16, b=bank: e.transpose(
                    b[:, 0:128], XA[:, 4096 + kc16 * 128:4096 + (kc16 + 1) * 128], ident_ap),
                    inc=s_tc, after=texps + ([et_free[ep]] if et_free[ep] else []))
                tev = k.emit("s", lambda e, b=bank, ep=ep: e.copy(
                    SMALL[:, 3200 + ep * 128:3200 + (ep + 1) * 128], b[:, 0:128]),
                    inc=s_sk, after=[ttr] + ([mmB] if kc16 >= 2 else []))
                mmA = k.emit("t", lambda e, ep=ep, kc16=kc16: e.matmul(
                    PS[6][:, 0:65], SMALL[:, 3200 + ep * 128:3200 + (ep + 1) * 128],
                    SMALL[:, 1024 + kc16 * 65:1024 + (kc16 + 1) * 65],
                    start=(kc16 == 0), stop=(kc16 == NT - 1)),
                    inc=None, after=[tev] + ([w1p_free] if (kc16 == 0 and w1p_free) else []))
                mmB = k.emit("t", lambda e, ep=ep, kc16=kc16: e.matmul(
                    PS[7][:, 0:65], SMALL[:, 3200 + ep * 128:3200 + (ep + 1) * 128],
                    SMALL[:, 2112 + kc16 * 65:2112 + (kc16 + 1) * 65],
                    start=(kc16 == 0), stop=(kc16 == NT - 1)),
                    inc=s_tc if kc16 == NT - 1 else None, after=())
                et_free[ep] = mmB
            pcol = STAT[:, 96 + qt:97 + qt]
            c1 = k.emit("v", lambda e: e.reciprocal(STAT[:, 16:17], PS[6][:, 64:65]),
                        inc=s_vc, after=[mmB])
            c2 = k.emit("v", lambda e: e.reciprocal(STAT[:, 17:18], PS[7][:, 64:65]),
                        inc=s_vc, after=[c1])
            c3 = k.emit("v", lambda e, pcol=pcol: e.tensor_copy(STAT[:, 18:19], pcol),
                        inc=s_vc, after=[c2, tok_masks])
            if qt == 0:
                c3 = k.emit("v", lambda e: e.memset(STAT[0:5, 18:19], 0.5), inc=s_vc, after=[c3])
            c4 = k.emit("v", lambda e: e.tensor_tensor(STAT[:, 16:17], STAT[:, 16:17],
                                                       STAT[:, 18:19], ALU.mult),
                        inc=s_vc, after=[c3])
            c5 = k.emit("v", lambda e: e.tensor_scalar(STAT[:, 19:20], STAT[:, 18:19],
                                                       -1.0, 1.0, ALU.mult, ALU.add),
                        inc=s_vc, after=[c4])
            if qt == 0:
                c5 = k.emit("v", lambda e: e.memset(STAT[0:5, 19:20], 0.5), inc=s_vc, after=[c5])
            c6 = k.emit("v", lambda e: e.tensor_tensor(STAT[:, 17:18], STAT[:, 17:18],
                                                       STAT[:, 19:20], ALU.mult),
                        inc=s_vc, after=[c5])
            c7 = k.emit("v", lambda e: e.tensor_scalar(XA[:, 6144:6208], PS[6][:, 0:64],
                                                       STAT[:, 16:17], None, ALU.mult),
                        inc=s_vc, after=[c6] + ([wstage_free] if wstage_free else []))
            c8 = k.emit("v", lambda e: e.scalar_tensor_tensor(
                SMALL[:, 3456:3520], PS[7][:, 0:64], STAT[:, 17:18], XA[:, 6144:6208],
                ALU.mult, ALU.add), inc=s_vc, after=[c7])
            w1p_free = c8
            bank = PS[4 + qt % 2]
            ttr = k.emit("t", lambda e, b=bank: e.transpose(
                b[0:64, 0:128], SMALL[:, 3456:3520], ident_ap), inc=s_tc,
                after=[c8] + [t for t in et_free if t])
            otev = k.emit("s", lambda e, b=bank: e.copy(XA[0:64, 6208:6336], b[0:64, 0:128]),
                          inc=s_sk, after=[ttr])
            wstage_free = otev
            for eo in range(2):
                par = eo
                k.dma("y", SMALL[0:64, 3520 + par * 512:3520 + (par + 1) * 512],
                      bass.AP(H["g1_wproj"], h * 64 * DM + eo * 512, [[DM, 64], [1, 512]]),
                      s_pw[par], after=(pw_free[par] or []) + tok_e0)
                tok_pw = s_pw[par].tok()
                bank2 = PS[2 + eo]
                mm = k.emit("t", lambda e, b=bank2, par=par: e.matmul(
                    b[:, :], XA[0:64, 6208:6336],
                    SMALL[0:64, 3520 + par * 512:3520 + (par + 1) * 512],
                    start=True, stop=True), inc=s_tc, after=[tok_pw, otev])
                padd = k.emit("v", lambda e, b=bank2, qt=qt, eo=eo: e.tensor_tensor(
                    RES[:, qt * DM + eo * 512:qt * DM + (eo + 1) * 512],
                    RES[:, qt * DM + eo * 512:qt * DM + (eo + 1) * 512],
                    b[:, :], ALU.add), inc=s_vc, after=[mm])
                pw_free[par] = [padd, s_tc.tok()]
        ql_free = [s_tc.tok(), s_vc.tok()]
    tok_f = barrier()
    k.dma("y", MID[:, 0:1024], bass.AP(H["g1_bproj"], 0, [[0, 128], [1, 1024]]), s_bp,
          after=tok_f)
    tok_bp = s_bp.tok()
    fin = []
    for tt in range(NT):
        fv = k.emit("v", lambda e, tt=tt: e.tensor_tensor(
            RES[:, tt * DM:(tt + 1) * DM], RES[:, tt * DM:(tt + 1) * DM],
            MID[:, 0:1024], ALU.add), inc=s_vc, after=[tok_bp])
        fin.append(fv)
    for tt in range(NT):
        k.dma("g", H["out"].ap()[tt * 128:(tt + 1) * 128, :], RES[:, tt * DM:(tt + 1) * DM],
              s_out, after=[fin[tt]])
    for e in ("g", "y", "t", "v", "s"):
        k.waits(e, [s_out.tok()])
